# version 07
# baseline (speedup 1.0000x reference)
import numpy as np
from contextlib import ExitStack

import concourse.bass as bass
import concourse.mybir as mybir
from concourse.bass_utils import run_bass_kernel_spmd
from concourse.alu_op_type import AluOpType as ALU
from concourse.aot_env import allow_ici_all_to_all

F32 = mybir.dt.float32
BF16 = mybir.dt.bfloat16
AF = mybir.ActivationFunctionType
AX = mybir.AxisListType

B, T, C = 4, 4096, 1024
TH = 2048
NH, DK = 8, 64
IFF = 2816
CH = 512
NCH = T // CH
STRIPW = 8320
QCW = 1024
TCH = 256
VROW = NH * 65 + 8
WGR = 768
WDR = 256
WSTR = 2 * WGR * C + WDR * IFF
DBG = ()


def build_nc():
    nc = bass.Bass("TRN2", target_bir_lowering=False, detect_race_conditions=False)
    es = ExitStack()

    def din(name, shape, dtype=F32):
        return nc.dram_tensor(name, shape, dtype, kind="ExternalInput").ap()

    x_full = din("x_full", [T, C])
    x_half = din("x_half", [17 * 128, C])
    eye_f = din("eye_f", [128, 128])
    wxt = din("wxt", [C, 1040])
    a_log = din("a_log", [1, 8])
    d_vec = din("d_vec", [128, 8])
    mur = din("mur", [128, 8])
    muk = din("muk", [128, 8])
    wrt = din("wrt", [C, C])
    wkt = din("wkt", [C, C])
    wvt = din("wvt", [C, C])
    woutt = din("woutt", [C, C])
    wqt = din("wqt", [C, 512])
    wktA = din("wktA", [C, 512])
    wvtA = din("wvtA", [C, 512])
    wot = din("wot", [512, C])
    strips = din("strips", [NH, 128, STRIPW])
    wg_sl = din("wg_sl", [WGR, C])
    wu_sl = din("wu_sl", [WGR, C])
    wd_sl = din("wd_sl", [WDR, IFF])

    y_out = nc.dram_tensor("y_out", [TH, C], F32, kind="ExternalOutput").ap()

    dbg = {}
    for name, shape in [
        ("dbg_xn0", [128, 8 * CH]), ("dbg_delta0", [128, 8 * CH]),
        ("dbg_ypart", [128, 8 * CH]), ("dbg_g1", [128, 8 * CH]),
        ("dbg_x1", [128, 8 * CH]), ("dbg_s2", [128, 8 * TCH]),
        ("dbg_xq", [128, 2 * C]), ("dbg_x2", [128, 8 * TCH]),
        ("dbg_x1c", [128, 8 * TCH]), ("dbg_x1nc", [128, 8 * TCH]),
        ("dbg_xtk", [128, 2 * C]), ("dbg_outn", [128, 4 * T]),
        ("dbg_qt2", [128, 4 * T]), ("dbg_kt2", [128, 4 * T]),
        ("dbg_x1f", [128, 8 * T]), ("dbg_agin", [128, 8 * TH]),
        ("dbg_x1b", [128, 8 * CH]),
        ("dbg_den", [NH, T]),
        ("dbg_y", [128, 8 * CH]),
        ("dbg_wg", [128, 8 * IFF]), ("dbg_wu", [128, 8 * IFF]),
        ("dbg_wd", [128, 22 * C]), ("dbg_h", [128, 2 * IFF]),
        ("dbg_hq", [128, 2 * IFF]), ("dbg_dn", [128, 8 * TCH]),
    ]:
        if name in DBG:
            dbg[name] = nc.dram_tensor(name, shape, F32, kind="ExternalOutput").ap()

    def dsc(name, shape, dtype=F32):
        return nc.dram_tensor(name, shape, dtype).ap()

    bc_dram = dsc("bc_dram", [16, T])
    ab_dram = dsc("ab_dram", [2, CH])
    rs_in = dsc("rs_in", [2 * 8 * 128 * TH])
    rs_out = dsc("rs_out", [8 * 128 * TH])
    g1_dram = dsc("g1_dram", [128, 8 * TH])
    x1_dram = dsc("x1_dram", [128, 8 * TH])
    ag_in = dsc("ag_in", [8 * 128 * TH], BF16)
    ag_out = dsc("ag_out", [2 * 8 * 128 * TH], BF16)
    den_dram = dsc("den_dram", [NH, T])
    s2rs_in = dsc("s2rs_in", [2 * 8 * 128 * TH])
    s2rs_out = dsc("s2rs_out", [8 * 128 * TH])
    wq_in = dsc("wq_in", [WSTR], BF16)
    wq_out = dsc("wq_out", [4 * WSTR], BF16)

    def sbt(name, shape, dtype=F32, stack=None):
        return (stack or es).enter_context(nc.sbuf_tensor(name, shape, dtype))

    def pst(name, shape, dtype=F32, stack=None):
        return (stack or es).enter_context(nc.psum_tensor(name, shape, dtype))

    eyef = sbt("eyef", [128, 128])
    eyeb = sbt("eyeb", [128, 128], BF16)
    ones_f = sbt("ones_f", [128, 1])
    c_eps = sbt("c_eps", [1, 1])
    c_one = sbt("c_one", [128, 1])
    a_ap = sbt("a_ap", [128, 8])
    d_half = sbt("d_half", [128, 8])
    mur_t = sbt("mur_t", [128, 8])
    muk_t = sbt("muk_t", [128, 8])

    class Ctr:
        def __init__(self, name):
            self.h = es.enter_context(nc.semaphore(name))
            self.v = 0
    qS, qG, qC, qV = Ctr("qS"), Ctr("qG"), Ctr("qC"), Ctr("qV")
    pV, pS, pT, pP, pQ = Ctr("pV"), Ctr("pS"), Ctr("pT"), Ctr("pP"), Ctr("pQ")
    ccc = Ctr("ccc")
    QOF = {"sync": qS, "gpsimd": qG, "scalar": qC, "vector": qV}

    block = es.enter_context(nc.Block())

    def sec(eng):
        return getattr(block, eng)

    def dma(e, eng, out, in_):
        q = QOF[eng]
        if q.v > 0:
            e.wait_ge(q.h, q.v)
        e.dma_start(out=out, in_=in_).then_inc(q.h, 16)
        q.v += 16

    def mark(ctr, instr):
        instr.then_inc(ctr.h, 1)
        ctr.v += 1
        return ctr.v

    def ap(t, off, dims):
        if hasattr(t, "tensor"):
            return bass.AP(t.tensor, t.offset + off, dims)
        return bass.AP(t, off, dims)

    def snap():
        return {c: c.v for c in (qS, qG, qC, qV, pV, pS, pT, pP, pQ, ccc)}

    def wait(e, sn):
        for c, v in sn.items():
            if v > 0:
                e.wait_ge(c.h, v)

    def phase_barrier():
        sn = snap()
        for eng in ("sync", "scalar", "vector", "tensor", "gpsimd"):
            @sec(eng)
            def _(e):
                wait(e, sn)

    @sec("sync")
    def _(e):
        dma(e, "sync", eyef[:], eye_f[:])
        dma(e, "sync", d_half[:], d_vec[:])
        dma(e, "sync", mur_t[:], mur[:])
        dma(e, "sync", muk_t[:], muk[:])
        dma(e, "sync", a_ap[:, 0:8], ap(a_log, 0, [[0, 128], [1, 8]]))
    s_ld = snap()

    @sec("gpsimd")
    def _(e):
        dma(e, "gpsimd", eyeb[:], eye_f[:])

    @sec("scalar")
    def _(e):
        wait(e, s_ld)
        mark(pS, e.activation(a_ap[:, 0:8], a_ap[:, 0:8], AF.Exp))
    s_aexp = pS.v

    @sec("vector")
    def _(e):
        e.memset(ones_f[:], 1.0)
        e.memset(c_eps[:], 1e-5)
        e.memset(c_one[:], 1.0)
        e.wait_ge(pS.h, s_aexp)
        e.tensor_scalar(a_ap[:, 0:8], a_ap[:, 0:8], -1.0, 0.0, ALU.mult, ALU.add)
        wait(e, s_ld)
        mark(pV, e.tensor_scalar(d_half[:], d_half[:], 0.5, 0.0, ALU.mult, ALU.add))

    LN = {}

    def make_ln(stack, tag, W, xsq=None, bf=False, single=False):
        LN["W"] = W
        LN["psA"] = pst("psA" + tag, [128, 128], F32, stack)
        LN["psB"] = LN["psA"] if single else pst("psB" + tag, [128, 128], F32, stack)
        if bf:
            LN["psAb"] = pst("psAb" + tag, [128, 128], BF16, stack)
            LN["psBb"] = LN["psAb"] if single else pst("psBb" + tag, [128, 128], BF16, stack)
        LN["psln"] = pst("psln" + tag, [1, 2 * W], F32, stack)
        LN["stats"] = sbt("stats" + tag, [1, 6 * W], F32, stack)
        LN["absc"] = sbt("absc" + tag, [128, W], F32, stack)
        LN["bbsc"] = sbt("bbsc" + tag, [128, W], F32, stack)
        LN["xsq"] = xsq if xsq is not None else sbt("xsq" + tag, [128, 8 * W], F32, stack)

    def emit_tr(jobs, fp32=True):
        idd = eyef if fp32 else eyeb
        kA, kB = ("psA", "psB") if fp32 else ("psAb", "psBb")
        sn = snap()
        evbase = pV.v
        trbase = pT.v
        single = LN[kA] is LN[kB]
        for k, (st_, so, sp, dt_, do, dp) in enumerate(jobs):
            psm = LN[kA] if (k % 2 == 0) else LN[kB]

            @sec("tensor")
            def _(e, k=k, psm=psm, st_=st_, so=so, sp=sp):
                if k == 0:
                    wait(e, sn)
                if single and k >= 1:
                    e.wait_ge(pV.h, evbase + k)
                elif k >= 2:
                    e.wait_ge(pV.h, evbase + k - 1)
                mark(pT, e.transpose(psm[0:128, 0:128],
                                     ap(st_, so, [[sp, 128], [1, 128]]), idd[:]))

            @sec("vector")
            def _(e, k=k, psm=psm, dt_=dt_, do=do, dp=dp):
                if k == 0:
                    wait(e, sn)
                e.wait_ge(pT.h, trbase + k + 1)
                mark(pV, e.tensor_copy(ap(dt_, do, [[dp, 128], [1, 128]]), psm[0:128, 0:128]))

    def emit_ln(xt_tile, xt_w, col0, xn_tile, xn_w, W):
        sn = snap()
        S = LN["W"]
        xq = LN["xsq"]

        @sec("scalar")
        def _(e):
            wait(e, sn)
            mark(pS, e.activation(
                ap(xq, 0, [[8 * S, 128], [S, 8], [1, W]]),
                ap(xt_tile, col0, [[xt_w * 8, 128], [xt_w, 8], [1, W]]),
                AF.Square))
        s_sq = pS.v

        @sec("tensor")
        def _(e):
            wait(e, sn)
            for ct in range(8):
                e.matmul(LN["psln"][0:1, 0:W], ones_f[:, 0:1],
                         ap(xt_tile, col0 + ct * xt_w, [[xt_w * 8, 128], [1, W]]),
                         start=(ct == 0), stop=(ct == 7))
            e.wait_ge(pS.h, s_sq)
            for ct in range(8):
                mm = e.matmul(LN["psln"][0:1, S:S + W], ones_f[:, 0:1],
                              ap(xq, ct * S, [[8 * S, 128], [1, W]]),
                              start=(ct == 0), stop=(ct == 7))
            mark(pT, mm)
        s_mm = pT.v

        @sec("scalar")
        def _(e):
            st = LN["stats"]
            e.wait_ge(pT.h, s_mm)
            e.copy(st[0:1, 0:W], LN["psln"][0:1, 0:W])
            mark(pS, e.copy(st[0:1, S:S + W], LN["psln"][0:1, S:S + W]))
        s_pse = pS.v

        @sec("vector")
        def _(e):
            st = LN["stats"]
            e.wait_ge(pS.h, s_pse)
            e.tensor_scalar(st[0:1, 0:W], st[0:1, 0:W], 1.0 / C, 0.0, ALU.mult, ALU.add)
            e.tensor_scalar(st[0:1, S:S + W], st[0:1, S:S + W], 1.0 / C, 0.0, ALU.mult, ALU.add)
            e.tensor_tensor(st[0:1, 2 * S:2 * S + W], st[0:1, 0:W], st[0:1, 0:W], ALU.mult)
            mark(pV, e.tensor_tensor(st[0:1, 3 * S:3 * S + W], st[0:1, S:S + W],
                                     st[0:1, 2 * S:2 * S + W], ALU.subtract))
        s_var = pV.v

        @sec("scalar")
        def _(e):
            st = LN["stats"]
            e.wait_ge(pV.h, s_var)
            e.activation(st[0:1, 4 * S:4 * S + W], st[0:1, 3 * S:3 * S + W], AF.Ln, bias=c_eps[:])
            mark(pS, e.activation(st[0:1, 4 * S:4 * S + W], st[0:1, 4 * S:4 * S + W], AF.Exp, scale=-0.5))
        s_rsv = pS.v

        @sec("vector")
        def _(e):
            st = LN["stats"]
            e.wait_ge(pS.h, s_rsv)
            mark(pV, e.tensor_tensor(st[0:1, 5 * S:5 * S + W], st[0:1, 0:W],
                                     st[0:1, 4 * S:4 * S + W], ALU.mult))
        s_b2 = pV.v

        @sec("scalar")
        def _(e):
            st = LN["stats"]
            e.wait_ge(pV.h, s_b2)
            dma(e, "scalar", ab_dram[0:1, 0:W], st[0:1, 4 * S:4 * S + W])
            dma(e, "scalar", ab_dram[1:2, 0:W], st[0:1, 5 * S:5 * S + W])
            e.wait_ge(qC.h, qC.v)
            dma(e, "scalar", LN["absc"][:, 0:W], ap(ab_dram, 0, [[0, 128], [1, W]]))
            dma(e, "scalar", LN["bbsc"][:, 0:W], ap(ab_dram, CH, [[0, 128], [1, W]]))
        s_abr = qC.v

        @sec("vector")
        def _(e):
            e.wait_ge(qC.h, s_abr)
            for ct in range(8):
                src = ap(xt_tile, col0 + ct * xt_w, [[xt_w * 8, 128], [1, W]])
                dst = ap(xn_tile, col0 + ct * xn_w, [[xn_w * 8, 128], [1, W]])
                e.tensor_tensor(dst, src, LN["absc"][:, 0:W], ALU.mult)
                mm = e.tensor_tensor(dst, dst, LN["bbsc"][:, 0:W], ALU.subtract)
            mark(pV, mm)

    def make_q(stack, F, tag):
        return {
            "v": sbt("qv" + tag, [128, F], F32, stack),
            "qa": sbt("qqa" + tag, [128, F], F32, stack),
            "t2": sbt("qt2" + tag, [128, F], F32, stack),
            "bm": sbt("qbm" + tag, [128, F // 16], F32, stack),
            "rs": sbt("qrs" + tag, [128, F // 16], F32, stack),
            "pitch": F // 16,
        }

    def emit_quant(e, q_, xin, F, out_bf):
        nb = F // 16
        pw = q_["pitch"]
        mark(pV, e.tensor_reduce(q_["bm"][:, 0:nb], xin.rearrange("p (b s) -> p b s", s=16),
                                 AX.X, ALU.max, apply_absolute_value=True))
        e.wait_ge(pV.h, pV.v)
        mark(pV, e.tensor_scalar(q_["bm"][:, 0:nb], q_["bm"][:, 0:nb], 1.0 / 6.0, 1e-12,
                                 ALU.mult, ALU.max))
        e.wait_ge(pV.h, pV.v)
        mark(pV, e.reciprocal(q_["rs"][:, 0:nb], q_["bm"][:, 0:nb]))
        e.wait_ge(pV.h, pV.v)
        rs_bc = ap(q_["rs"], 0, [[pw, 128], [1, nb], [0, 16]])
        sc_bc = ap(q_["bm"], 0, [[pw, 128], [1, nb], [0, 16]])
        e.tensor_tensor(q_["v"][:, 0:F].rearrange("p (b s) -> p b s", s=16),
                        xin.rearrange("p (b s) -> p b s", s=16), rs_bc, ALU.mult)
        e.tensor_scalar(q_["qa"][:, 0:F], q_["v"][:, 0:F], 1e30, 1.0, ALU.mult, ALU.min)
        e.tensor_scalar(q_["qa"][:, 0:F], q_["qa"][:, 0:F], -1.0, 0.0, ALU.max, ALU.add)
        e.tensor_tensor(q_["v"][:, 0:F], q_["v"][:, 0:F], q_["qa"][:, 0:F], ALU.mult)
        av = q_["v"][:, 0:F]
        e.tensor_scalar(q_["t2"][:, 0:F], av, 0.25, 1.0, ALU.is_gt, ALU.mult)
        for m in (0.75, 1.25, 1.75):
            e.scalar_tensor_tensor(q_["t2"][:, 0:F], av, m, q_["t2"][:, 0:F], ALU.is_gt, ALU.add)
        e.tensor_scalar(q_["t2"][:, 0:F], q_["t2"][:, 0:F], 0.5, 0.0, ALU.mult, ALU.add)
        for m in (2.5, 3.5, 5.0, 5.0):
            e.scalar_tensor_tensor(q_["t2"][:, 0:F], av, m, q_["t2"][:, 0:F], ALU.is_gt, ALU.add)
        e.tensor_tensor(q_["t2"][:, 0:F], q_["t2"][:, 0:F], q_["qa"][:, 0:F], ALU.mult)
        return e.tensor_tensor(out_bf.rearrange("p (b s) -> p b s", s=16),
                               q_["t2"][:, 0:F].rearrange("p (b s) -> p b s", s=16),
                               sc_bc, ALU.mult)

    qs = ExitStack()
    q_in = sbt("q_in", [128, IFF], F32, qs)
    q_outt = sbt("q_outt", [128, IFF], BF16, qs)
    QW = make_q(qs, IFF, "w")

    wjobs = []
    off = 0
    for (mat, rows, fdim) in [(wg_sl, WGR, C), (wu_sl, WGR, C), (wd_sl, WDR, IFF)]:
        r = 0
        while r < rows:
            rr = min(128, rows - r)
            wjobs.append((mat, r, rr, fdim, off))
            off += rr * fdim
            r += rr
    assert off == WSTR

    for (mat, r, rr, fdim, woff) in wjobs:
        @sec("gpsimd")
        def _(e, mat=mat, r=r, rr=rr, fdim=fdim, wv=pV.v, wq2=qG.v):
            e.wait_ge(pV.h, wv)
            e.wait_ge(qG.h, wq2)
            dma(e, "gpsimd", q_in[0:rr, 0:fdim], mat[r:r + rr, :])
        gq = qG.v

        @sec("vector")
        def _(e, rr=rr, fdim=fdim, gq=gq):
            e.wait_ge(qG.h, gq)
            mark(pV, emit_quant(e, QW, q_in[0:rr, 0:fdim], fdim, q_outt[0:rr, 0:fdim]))
        vq = pV.v

        @sec("gpsimd")
        def _(e, rr=rr, fdim=fdim, woff=woff, vq=vq):
            e.wait_ge(pV.h, vq)
            dma(e, "gpsimd", ap(wq_in, woff, [[fdim, rr], [1, fdim]]), q_outt[0:rr, 0:fdim])

    @sec("gpsimd")
    def _(e, gw=qG.v):
        e.wait_ge(qG.h, gw)
        e.collective_compute(
            "AllGather", ALU.bypass, replica_groups=[[0, 2, 4, 6], [1, 3, 5, 7]],
            ins=[wq_in.opt()], outs=[wq_out.opt()],
        ).then_inc(ccc.h, 1)
    ccc.v += 1
    cc_wq = ccc.v
    qs.close()

    lns = ExitStack()
    make_ln(lns, "0", CH)
    l0 = ExitStack()
    xtok = sbt("xtok", [128, 4 * C], F32, l0)
    xT = sbt("xT", [128, 8 * CH], F32, l0)
    xnT = sbt("xnT", [128, 8 * CH], F32, l0)
    xnb = sbt("xnb", [128, 8 * CH], BF16, l0)
    deltach = sbt("deltach", [128, 8 * CH], F32, l0)
    dxch = xtok
    bcsb = sbt("bcsb", [16, CH], F32, l0)
    bbc = sbt("bbc", [128, 8 * CH], F32, l0)
    cbc = sbt("cbc", [128, 8 * CH], F32, l0)
    dAex0 = sbt("dAex0", [128, 8 * CH], F32, l0)
    dAex1 = sbt("dAex1", [128, 8 * CH], F32, l0)
    uu = sbt("uu", [128, CH], F32, l0)
    hh = sbt("hh", [128, CH], F32, l0)
    acc0 = sbt("acc0", [128, CH], F32, l0)
    acc1 = sbt("acc1", [128, CH], F32, l0)
    yt0 = sbt("yt0", [128, CH], F32, l0)
    yt1 = sbt("yt1", [128, CH], F32, l0)
    carry = sbt("carry", [128, 64], F32, l0)
    wxtb = sbt("wxtb", [128, 8 * 1040], BF16, l0)
    psxp0 = pst("psxp0", [128, CH], F32, l0)
    psxp1 = pst("psxp1", [128, CH], F32, l0)
    psbc = pst("psbc", [16, CH], F32, l0)

    @sec("gpsimd")
    def _(e):
        dma(e, "gpsimd", wxtb[:].rearrange("p (a m) -> p a m", a=8),
            wxt.rearrange("(a p) m -> p a m", p=128))
    g_wx = qG.v

    @sec("vector")
    def _(e):
        mark(pV, e.memset(carry[:], 0.0))

    for ch in range(NCH):
        phase_barrier()
        t0 = ch * CH
        sn_pT = pT.v
        qs_guard = qS.v

        @sec("sync")
        def _(e, t0=t0, w=sn_pT):
            e.wait_ge(pT.h, w)
            dma(e, "sync", xtok[:].rearrange("p (a c) -> p a c", a=4),
                x_full[t0:t0 + CH, :].rearrange("(a p) c -> p a c", p=128))

        @sec("vector")
        def _(e, w=qs_guard):
            e.wait_ge(qS.h, w)
        emit_tr([(xtok, tt * C + ct * 128, 4 * C, xT, ct * CH + tt * 128, 8 * CH)
                 for tt in range(4) for ct in range(8)], fp32=True)
        emit_ln(xT, CH, 0, xnT, CH, CH)
        sn_xn = pV.v

        @sec("vector")
        def _(e, w=sn_xn):
            e.wait_ge(pV.h, w)
            mark(pV, e.tensor_copy(xnb[:], xnT[:]))
        s_xnb = pV.v

        if "dbg_xn0" in dbg and ch == 0:
            @sec("gpsimd")
            def _(e, w=s_xnb):
                e.wait_ge(pV.h, w)
                dma(e, "gpsimd", dbg["dbg_xn0"][:, :], xnT[:])

        sp_marks = []
        bc_ev = [0]
        for m in range(9):
            psm = psxp0 if (m % 2 == 0) else psxp1

            @sec("tensor")
            def _(e, m=m, psm=psm, wxn=s_xnb, wld=g_wx):
                if m == 0:
                    e.wait_ge(qG.h, wld)
                    e.wait_ge(pV.h, wxn)
                if m >= 2:
                    e.wait_ge(pS.h, sp_marks[m - 2]) if m - 2 < len(sp_marks) else None
                for ct in range(8):
                    if m < 8:
                        mm = e.matmul(psm[:], wxtb[:, ct * 1040 + m * 128: ct * 1040 + (m + 1) * 128],
                                      xnb[:, ct * CH:(ct + 1) * CH], start=(ct == 0), stop=(ct == 7))
                    else:
                        mm = e.matmul(psbc[:], wxtb[:, ct * 1040 + 1024: ct * 1040 + 1040],
                                      xnb[:, ct * CH:(ct + 1) * CH], start=(ct == 0), stop=(ct == 7))
                mark(pT, mm)
            s_mm = pT.v

            if m < 8:
                @sec("scalar")
                def _(e, m=m, psm=psm, w=s_mm):
                    e.wait_ge(pT.h, w)
                    mark(pS, e.activation(deltach[:, m * CH:(m + 1) * CH], psm[:], AF.Exp))
                sp_marks.append(pS.v)
            else:
                @sec("scalar")
                def _(e, w=s_mm):
                    e.wait_ge(pT.h, w)
                    mark(pS, e.copy(bcsb[:], psbc[:]))
                bc_ev[0] = pS.v

        @sec("scalar")
        def _(e, w=sp_marks[-1]):
            e.wait_ge(pS.h, w)
            mark(pS, e.activation(deltach[:], deltach[:], AF.Ln, bias=c_one[:]))
        s_sp = pS.v

        if "dbg_delta0" in dbg and ch == 0:
            @sec("gpsimd")
            def _(e, w=s_sp):
                e.wait_ge(pS.h, w)
                dma(e, "gpsimd", dbg["dbg_delta0"][:, :], deltach[:])

        @sec("scalar")
        def _(e, w=bc_ev[0], t0=t0):
            e.wait_ge(pS.h, w)
            dma(e, "scalar", bc_dram[:, t0:t0 + CH], bcsb[:])
        s_bcw = qC.v

        @sec("scalar")
        def _(e, w=s_bcw, t0=t0, wv=pV.v):
            e.wait_ge(qC.h, w)
            e.wait_ge(pV.h, wv)
            dma(e, "scalar", bbc[:].rearrange("p (n t) -> p n t", n=8),
                ap(bc_dram, t0, [[0, 128], [T, 8], [1, CH]]))
            dma(e, "scalar", cbc[:].rearrange("p (n t) -> p n t", n=8),
                ap(bc_dram, 8 * T + t0, [[0, 128], [T, 8], [1, CH]]))
        s_bcr = qC.v

        @sec("vector")
        def _(e, w=s_sp):
            e.wait_ge(pS.h, w)
            mark(pV, e.tensor_tensor(dxch[:], deltach[:], xnT[:], ALU.mult))
        s_dx = pV.v

        scan_done = []
        ydma = []
        hb, cloc = ch // 4, (ch % 4) * CH

        def emit_scan(ct, wda):
            sl = slice(ct * CH, (ct + 1) * CH)
            dAx = dAex0 if (ct % 2 == 0) else dAex1
            yt = yt0 if (ct % 2 == 0) else yt1

            @sec("vector")
            def _(e, ct=ct, dAx=dAx, yt=yt, sl=sl, wda=wda, wbc=s_bcr,
                  wdx=s_dx, wdm=(ydma[ct - 2] if ct >= 2 else 0)):
                e.wait_ge(pS.h, wda)
                e.wait_ge(qC.h, wbc)
                e.wait_ge(pV.h, wdx)
                if wdm:
                    e.wait_ge(qS.h, wdm)
                for n in range(8):
                    e.tensor_tensor(uu[:], dxch[:, sl], bbc[:, n * CH:(n + 1) * CH], ALU.mult)
                    e.tensor_tensor_scan(hh[:], dAx[:, n * CH:(n + 1) * CH], uu[:],
                                         carry[:, ct * 8 + n: ct * 8 + n + 1],
                                         ALU.mult, ALU.add)
                    if n == 0:
                        e.tensor_tensor(acc0[:], hh[:], cbc[:, 0:CH], ALU.mult)
                    else:
                        src, dst = (acc0, acc1) if (n % 2 == 1) else (acc1, acc0)
                        e.tensor_tensor(uu[:], hh[:], cbc[:, n * CH:(n + 1) * CH], ALU.mult)
                        e.tensor_tensor(dst[:], src[:], uu[:], ALU.add)
                    e.tensor_copy(carry[:, ct * 8 + n: ct * 8 + n + 1], hh[:, CH - 1:CH])
                mark(pV, e.scalar_tensor_tensor(yt[:], xnT[:, sl],
                                                d_half[:, ct:ct + 1], acc1[:],
                                                ALU.mult, ALU.add))
            scan_done.append(pV.v)

            @sec("sync")
            def _(e, ct=ct, yt=yt, hb=hb, cloc=cloc, w=scan_done[ct]):
                e.wait_ge(pV.h, w)
                dma(e, "sync", ap(rs_in, hb * (8 * 128 * TH) + cloc + ct * 128 * TH,
                                  [[TH, 128], [1, CH]]), yt[:])
            ydma.append(qS.v)

        for cp in range(4):
            for ct in (2 * cp, 2 * cp + 1):
                dAx = dAex0 if (ct % 2 == 0) else dAex1
                vguard = scan_done[ct - 2] if ct >= 2 else 0

                @sec("gpsimd")
                def _(e, ct=ct, dAx=dAx, w=s_sp, vg=vguard):
                    e.wait_ge(pS.h, w)
                    if vg:
                        e.wait_ge(pV.h, vg)
                    mark(pP, e.tensor_tensor(
                        ap(dAx, 0, [[8 * CH, 128], [CH, 8], [1, CH]]),
                        ap(deltach, ct * CH, [[8 * CH, 128], [0, 8], [1, CH]]),
                        ap(a_ap, 0, [[8, 128], [1, 8], [0, CH]]), ALU.mult))
            s_pm = pP.v

            @sec("scalar")
            def _(e, w=s_pm):
                e.wait_ge(pP.h, w)
                e.activation(dAex0[:], dAex0[:], AF.Exp)
                mark(pS, e.activation(dAex1[:], dAex1[:], AF.Exp))
            da_pair = pS.v
            emit_scan(2 * cp, da_pair)
            emit_scan(2 * cp + 1, da_pair)
        s_y = pV.v

    l0.close()
    phase_barrier()

    rk = ExitStack()
    xtok2 = sbt("xtok2", [128, 4 * C], F32, rk)
    xTb = sbt("xTb", [128, 8 * CH], F32, rk)
    xnbuf0 = sbt("xnbuf0", [128, 8 * CH], F32, rk)
    xnbuf1 = sbt("xnbuf1", [128, 8 * CH], F32, rk)
    mixt = sbt("mixt", [128, CH], F32, rk)
    xrb = sbt("xrb", [128, 8 * CH], BF16, rk)
    xkb = sbt("xkb", [128, 8 * CH], BF16, rk)
    rT = sbt("rT", [128, 8 * CH], F32, rk)
    kq = sbt("kq", [128, 8 * CH], F32, rk)
    kqb = sbt("kqb", [128, 8 * CH], BF16, rk)
    gtm = sbt("gtm", [128, CH], F32, rk)
    vsb = sbt("vsb", [128, CH], F32, rk)
    wrtb = sbt("wrtb", [128, 8 * C], BF16, rk)
    wktb = sbt("wktb", [128, 8 * C], BF16, rk)
    wvtb = sbt("wvtb", [128, 8 * C], BF16, rk)
    psR0 = pst("psR0", [128, CH], F32, rk)
    psR1 = pst("psR1", [128, CH], F32, rk)

    @sec("gpsimd")
    def _(e):
        for (dst, src) in [(wrtb, wrt), (wktb, wkt), (wvtb, wvt)]:
            dma(e, "gpsimd", dst[:].rearrange("p (a m) -> p a m", a=8),
                src.rearrange("(a p) m -> p a m", p=128))
    g_wr = qG.v

    def lnblock(blk):
        phase_barrier()
        buf = xnbuf0 if (blk % 2 == 0) else xnbuf1
        r0 = blk * CH
        nrows = max(0, min(CH, 17 * 128 - r0))
        ntt = (nrows + 127) // 128
        sn_pT = pT.v

        @sec("sync")
        def _(e, r0=r0, ntt=ntt, w=sn_pT):
            e.wait_ge(pT.h, w)
            if ntt > 0:
                dma(e, "sync", xtok2[:, 0:ntt * C].rearrange("p (a c) -> p a c", a=ntt),
                    x_half[r0:r0 + ntt * 128, :].rearrange("(a p) c -> p a c", p=128))
        if ntt < 4:
            sn0 = snap()

            @sec("vector")
            def _(e, sn0=sn0, ntt=ntt):
                wait(e, sn0)
                mark(pV, e.memset(xtok2[:, ntt * C:4 * C], 0.0))
        emit_tr([(xtok2, tt * C + ct * 128, 4 * C, xTb, ct * CH + tt * 128, 8 * CH)
                 for tt in range(4) for ct in range(8)], fp32=True)
        emit_ln(xTb, CH, 0, buf, CH, CH)
        return buf

    b_prev = lnblock(0)
    for tc in range(4):
        b_cur = b_prev
        b_next = lnblock(tc + 1)
        phase_barrier()
        sn0 = snap()

        @sec("vector")
        def _(e, b_cur=b_cur, b_next=b_next, sn0=sn0):
            wait(e, sn0)
            for ct in range(8):
                cs = ct * CH
                e.tensor_tensor(mixt[:, 0:CH - 1], b_cur[:, cs + 1:cs + CH],
                                b_cur[:, cs:cs + CH - 1], ALU.subtract)
                e.tensor_tensor(mixt[:, CH - 1:CH], b_next[:, cs:cs + 1],
                                b_cur[:, cs + CH - 1:cs + CH], ALU.subtract)
                e.scalar_tensor_tensor(xrb[:, cs:cs + CH], mixt[:, 0:CH],
                                       mur_t[:, ct:ct + 1], b_cur[:, cs:cs + CH],
                                       ALU.mult, ALU.add)
                mm = e.scalar_tensor_tensor(xkb[:, cs:cs + CH], mixt[:, 0:CH],
                                            muk_t[:, ct:ct + 1], b_cur[:, cs:cs + CH],
                                            ALU.mult, ALU.add)
            mark(pV, mm)
        s_mix = pV.v

        for m in range(8):
            psm = psR0 if (m % 2 == 0) else psR1

            @sec("tensor")
            def _(e, m=m, psm=psm, w=s_mix, wld=g_wr, ws=pS.v):
                if m == 0:
                    e.wait_ge(pV.h, w)
                    e.wait_ge(qG.h, wld)
                else:
                    e.wait_ge(pS.h, ws)
                for ct in range(8):
                    mm = e.matmul(psm[:], wrtb[:, ct * C + m * 128: ct * C + (m + 1) * 128],
                                  xrb[:, ct * CH:(ct + 1) * CH], start=(ct == 0), stop=(ct == 7))
                mark(pT, mm)
            s_rm = pT.v

            @sec("scalar")
            def _(e, m=m, psm=psm, w=s_rm):
                e.wait_ge(pT.h, w)
                mark(pS, e.activation(rT[:, m * CH:(m + 1) * CH], psm[:], AF.Sigmoid))
        s_rsig = pS.v
        for m in range(8):
            psm = psR0 if (m % 2 == 0) else psR1

            @sec("tensor")
            def _(e, m=m, psm=psm, ws=pS.v, w0=s_rsig):
                if m < 2:
                    e.wait_ge(pS.h, w0)
                else:
                    e.wait_ge(pS.h, ws)
                for ct in range(8):
                    mm = e.matmul(psm[:], wktb[:, ct * C + m * 128: ct * C + (m + 1) * 128],
                                  xkb[:, ct * CH:(ct + 1) * CH], start=(ct == 0), stop=(ct == 7))
                mark(pT, mm)
            s_km = pT.v

            @sec("scalar")
            def _(e, m=m, psm=psm, w=s_km):
                e.wait_ge(pT.h, w)
                mark(pS, e.activation(kq[:, m * CH:(m + 1) * CH], psm[:], AF.Relu))
        s_kr = pS.v

        @sec("scalar")
        def _(e, w=s_kr):
            e.wait_ge(pS.h, w)
            mark(pS, e.activation(kq[:], kq[:], AF.Square))
        s_ksq = pS.v

        @sec("vector")
        def _(e, w=s_ksq):
            e.wait_ge(pS.h, w)
            mark(pV, e.tensor_copy(kqb[:], kq[:]))
        s_kqb = pV.v

        for m in range(8):
            psm = psR0 if (m % 2 == 0) else psR1

            @sec("tensor")
            def _(e, m=m, psm=psm, w=s_kqb, ws=pS.v):
                if m < 2:
                    e.wait_ge(pV.h, w)
                else:
                    e.wait_ge(pS.h, ws)
                for ct in range(8):
                    mm = e.matmul(psm[:], wvtb[:, ct * C + m * 128: ct * C + (m + 1) * 128],
                                  kqb[:, ct * CH:(ct + 1) * CH], start=(ct == 0), stop=(ct == 7))
                mark(pT, mm)
            s_vm = pT.v

            @sec("scalar")
            def _(e, m=m, psm=psm, w=s_vm):
                e.wait_ge(pT.h, w)
                mark(pS, e.copy(vsb[:], psm[:]))
            s_vev = pS.v

            @sec("vector")
            def _(e, m=m, w=s_vev, wq=qS.v):
                e.wait_ge(pS.h, w)
                e.wait_ge(qS.h, wq)
                mark(pV, e.tensor_tensor(gtm[:], rT[:, m * CH:(m + 1) * CH], vsb[:], ALU.mult))
            s_rv = pV.v

            @sec("scalar")
            def _(e, w=s_rv):
                e.wait_ge(pV.h, w)
                mark(pS, e.activation(gtm[:], gtm[:], AF.Silu))
            s_g = pS.v

            @sec("sync")
            def _(e, tc=tc, m=m, w=s_g):
                e.wait_ge(pS.h, w)
                dma(e, "sync", ap(g1_dram, m * TH + tc * CH, [[8 * TH, 128], [1, CH]]), gtm[:])
            if "dbg_g1" in dbg and tc == 0:
                @sec("gpsimd")
                def _(e, m=m, w=s_g):
                    e.wait_ge(pS.h, w)
                    dma(e, "gpsimd", ap(dbg["dbg_g1"], m * CH, [[8 * CH, 128], [1, CH]]), gtm[:])
        b_prev = b_next

    rk.close()
    phase_barrier()

    p2 = ExitStack()
    yb = sbt("yb", [128, 8 * CH], BF16, p2)
    g1c = sbt("g1c", [128, 8 * CH], F32, p2)
    xTh = sbt("xTh", [128, 8 * CH], F32, p2)
    x1T = sbt("x1T", [128, 8 * CH], F32, p2)
    x1n = sbt("x1n", [128, 8 * CH], F32, p2)
    x1nb = sbt("x1nb", [128, 8 * CH], BF16, p2)
    tmp2 = sbt("tmp2", [128, CH], F32, p2)
    woutb = sbt("woutb", [128, 8 * C], BF16, p2)
    xtok3 = sbt("xtok3", [128, 4 * C], F32, p2)
    s1sb = sbt("s1sb", [128, CH], F32, p2)
    psW = pst("psW", [128, CH], F32, p2)

    @sec("gpsimd")
    def _(e, w=qS.v):
        e.wait_ge(qS.h, w)
        e.collective_compute(
            "ReduceScatter", ALU.add, replica_groups=[[0, 1], [2, 3], [4, 5], [6, 7]],
            ins=[rs_in.opt()], outs=[rs_out.opt()],
        ).then_inc(ccc.h, 1)
        dma(e, "gpsimd", woutb[:].rearrange("p (a m) -> p a m", a=8),
            woutt.rearrange("(a p) m -> p a m", p=128))
    ccc.v += 1
    cc_y = ccc.v
    g_wo = qG.v

    if "dbg_y" in dbg:
        @sec("gpsimd")
        def _(e, w=cc_y):
            e.wait_ge(ccc.h, w)
            dma(e, "gpsimd", dbg["dbg_y"][:, :].rearrange("p (c t) -> p c t", c=8),
                ap(rs_out, 0, [[TH, 128], [128 * TH, 8], [1, CH]]))

    for tc in range(4):
        phase_barrier()
        cl = tc * CH
        sn_v = pV.v

        @sec("gpsimd")
        def _(e, cl=cl, w=cc_y, wv=sn_v):
            e.wait_ge(ccc.h, w)
            e.wait_ge(pV.h, wv)
            dma(e, "gpsimd", yb[:].rearrange("p (c t) -> p c t", c=8),
                ap(rs_out, cl, [[TH, 128], [128 * TH, 8], [1, CH]]))
            dma(e, "gpsimd", g1c[:].rearrange("p (c t) -> p c t", c=8),
                ap(g1_dram, cl, [[8 * TH, 128], [TH, 8], [1, CH]]))
        s_ybl = qG.v
        sn_pT = pT.v

        @sec("sync")
        def _(e, cl=cl, w=sn_pT):
            e.wait_ge(pT.h, w)
            dma(e, "sync", xtok3[:].rearrange("p (a c) -> p a c", a=4),
                x_half[1 + cl:1 + cl + CH, :].rearrange("(a p) c -> p a c", p=128))
        emit_tr([(xtok3, tt * C + ct * 128, 4 * C, xTh, ct * CH + tt * 128, 8 * CH)
                 for tt in range(4) for ct in range(8)], fp32=True)
        s_xTh = pV.v

        for m in range(8):
            @sec("tensor")
            def _(e, m=m, w=s_ybl, wld=g_wo, ws=pS.v):
                if m == 0:
                    e.wait_ge(qG.h, max(w, wld))
                e.wait_ge(pS.h, ws)
                for ct in range(8):
                    mm = e.matmul(psW[:], woutb[:, ct * C + m * 128: ct * C + (m + 1) * 128],
                                  yb[:, ct * CH:(ct + 1) * CH], start=(ct == 0), stop=(ct == 7))
                mark(pT, mm)
            s_wm = pT.v

            @sec("scalar")
            def _(e, m=m, w=s_wm, wv=pV.v):
                e.wait_ge(pT.h, w)
                e.wait_ge(pV.h, wv)
                mark(pS, e.copy(s1sb[:], psW[:]))
            s_ev = pS.v

            @sec("vector")
            def _(e, m=m, w=s_ev, wx=s_xTh, wq=qS.v):
                e.wait_ge(pS.h, w)
                e.wait_ge(pV.h, wx)
                if m == 0:
                    e.wait_ge(qS.h, wq)
                e.tensor_tensor(tmp2[:], s1sb[:], g1c[:, m * CH:(m + 1) * CH], ALU.mult)
                mark(pV, e.tensor_tensor(x1T[:, m * CH:(m + 1) * CH], tmp2[:],
                                         xTh[:, m * CH:(m + 1) * CH], ALU.add))
        s_x1 = pV.v

        @sec("sync")
        def _(e, cl=cl, w=s_x1):
            e.wait_ge(pV.h, w)
            dma(e, "sync", ap(x1_dram, cl, [[8 * TH, 128], [TH, 8], [1, CH]]),
                x1T[:].rearrange("p (c t) -> p c t", c=8))
        if "dbg_x1" in dbg and tc == 0:
            @sec("gpsimd")
            def _(e, w=s_x1):
                e.wait_ge(pV.h, w)
                dma(e, "gpsimd", dbg["dbg_x1"][:, :], x1T[:])

        emit_ln(x1T, CH, 0, x1n, CH, CH)
        s_ln1 = pV.v

        @sec("vector")
        def _(e, cl=cl, w=s_ln1, wq=qS.v):
            e.wait_ge(pV.h, w)
            e.wait_ge(qS.h, wq)
            mark(pV, e.tensor_copy(x1nb[:], x1n[:]))
        s_x1nb = pV.v

        @sec("sync")
        def _(e, cl=cl, w=s_x1nb):
            e.wait_ge(pV.h, w)
            dma(e, "sync", ap(ag_in, cl, [[TH, 128], [128 * TH, 8], [1, CH]]),
                x1nb[:].rearrange("p (c t) -> p c t", c=8))

    if "dbg_x1b" in dbg:
        @sec("gpsimd")
        def _(e, w=qS.v):
            e.wait_ge(qS.h, w)
            dma(e, "gpsimd", dbg["dbg_x1b"][:, :].rearrange("p (c t) -> p c t", c=8),
                ap(x1_dram, CH, [[8 * TH, 128], [TH, 8], [1, CH]]))
            e.wait_ge(qG.h, qG.v)

    if "dbg_agin" in dbg:
        @sec("gpsimd")
        def _(e, w=qS.v):
            e.wait_ge(qS.h, w)
            dma(e, "gpsimd", dbg["dbg_agin"][:, :].rearrange("p (c t) -> p c t", c=8),
                ap(ag_in, 0, [[TH, 128], [128 * TH, 8], [1, TH]]))
            e.wait_ge(qG.h, qG.v)

    @sec("gpsimd")
    def _(e, w=qS.v):
        e.wait_ge(qS.h, w)
        e.collective_compute(
            "AllGather", ALU.bypass, replica_groups=[[0, 1], [2, 3], [4, 5], [6, 7]],
            ins=[ag_in.opt()], outs=[ag_out.opt()],
        ).then_inc(ccc.h, 1)
    ccc.v += 1
    cc_x1n = ccc.v
    p2.close()
    lns.close()
    phase_barrier()

    at = ExitStack()
    QT = sbt("QT", [128, 4 * T], BF16, at)
    KT = sbt("KT", [128, 4 * T], BF16, at)
    Vb = sbt("Vb", [128, 32 * VROW], BF16, at)
    outN = sbt("outN", [128, 4 * T], BF16, at)

    atp = ExitStack()
    x1f = sbt("x1f", [128, 8 * T], BF16, atp)
    wAb = sbt("wAb", [128, 8 * 512], BF16, atp)
    psQ0 = pst("psQ0", [128, CH], F32, atp)
    psQ1 = pst("psQ1", [128, CH], F32, atp)

    @sec("gpsimd")
    def _(e, w=cc_x1n):
        dma(e, "gpsimd", wAb[:].rearrange("p (a m) -> p a m", a=8),
            wqt.rearrange("(a p) m -> p a m", p=128))
        e.wait_ge(ccc.h, w)
        for r in range(2):
            dma(e, "gpsimd", ap(x1f, r * TH, [[8 * T, 128], [T, 8], [1, TH]]),
                ap(ag_out, r * (8 * 128 * TH), [[TH, 128], [128 * TH, 8], [1, TH]]))
    g_at = qG.v

    for dst_t, wsrc in ((QT, wqt), (KT, wktA)):
        if dst_t is KT:
            @sec("gpsimd")
            def _(e, wsrc=wsrc, wt=pT.v):
                e.wait_ge(pT.h, wt)
                dma(e, "gpsimd", wAb[:].rearrange("p (a m) -> p a m", a=8),
                    wsrc.rearrange("(a p) m -> p a m", p=128))
        g_w = qG.v
        for m in range(4):
            for tn in range(8):
                @sec("tensor")
                def _(e, dst_t=dst_t, m=m, tn=tn, w=g_at, w2=g_w, ws=pS.v):
                    if m == 0 and tn == 0:
                        e.wait_ge(qG.h, max(w, w2))
                    e.wait_ge(pS.h, ws)
                    for ct in range(8):
                        mm = e.matmul(psQ0[:], wAb[:, ct * 512 + m * 128: ct * 512 + (m + 1) * 128],
                                      x1f[:, ct * T + tn * CH: ct * T + (tn + 1) * CH],
                                      start=(ct == 0), stop=(ct == 7))
                    mark(pT, mm)
                s_q = pT.v

                @sec("scalar")
                def _(e, dst_t=dst_t, m=m, tn=tn, w=s_q):
                    e.wait_ge(pT.h, w)
                    mark(pS, e.copy(dst_t[:, m * T + tn * CH: m * T + (tn + 1) * CH], psQ0[:]))

    @sec("gpsimd")
    def _(e, wt=pT.v):
        e.wait_ge(pT.h, wt)
        dma(e, "gpsimd", wAb[:].rearrange("p (a m) -> p a m", a=8),
            wvtA.rearrange("(a p) m -> p a m", p=128))
    g_wv = qG.v

    if "dbg_x1f" in dbg:
        @sec("gpsimd")
        def _(e, w=g_at):
            e.wait_ge(qG.h, w)
            dma(e, "gpsimd", dbg["dbg_x1f"][:, :], x1f[:])

    if "dbg_qt2" in dbg:
        @sec("gpsimd")
        def _(e, w=pV.v):
            e.wait_ge(pV.h, w)
            dma(e, "gpsimd", dbg["dbg_qt2"][:, :], QT[:])
            dma(e, "gpsimd", dbg["dbg_kt2"][:, :], KT[:])

    @sec("vector")
    def _(e):
        mark(pV, e.memset(Vb[:], 1.0))
    s_vmem = pV.v
    for tt in range(32):
        for hd in range(NH):
            @sec("tensor")
            def _(e, tt=tt, hd=hd, w=g_wv, ws=pS.v, wm=s_vmem):
                if tt == 0 and hd == 0:
                    e.wait_ge(qG.h, w)
                    e.wait_ge(pV.h, wm)
                e.wait_ge(pS.h, ws)
                for ct in range(8):
                    mm = e.matmul(psQ0[:, 0:64],
                                  x1f[:, ct * T + tt * 128: ct * T + (tt + 1) * 128],
                                  wAb[:, ct * 512 + hd * 64: ct * 512 + (hd + 1) * 64],
                                  start=(ct == 0), stop=(ct == 7))
                mark(pT, mm)
            s_v = pT.v

            @sec("scalar")
            def _(e, tt=tt, hd=hd, w=s_v):
                e.wait_ge(pT.h, w)
                mark(pS, e.copy(Vb[:, tt * VROW + hd * 65: tt * VROW + hd * 65 + 64],
                                psQ0[:, 0:64]))
    atp.close()
    phase_barrier()

    atm = ExitStack()
    strip0 = sbt("strip0", [128, STRIPW], BF16, atm)
    strip1 = sbt("strip1", [128, STRIPW], BF16, atm)
    PT0 = sbt("PT0", [128, QCW], BF16, atm)
    PT1 = sbt("PT1", [128, QCW], BF16, atm)
    PT2 = sbt("PT2", [128, QCW], BF16, atm)
    PT3 = sbt("PT3", [128, QCW], BF16, atm)
    den_sm = sbt("den_sm", [1, QCW], F32, atm)
    rbc = sbt("rbc", [128, T], F32, atm)
    psS0 = pst("psS0", [128, QCW], F32, atm)
    psS1 = pst("psS1", [128, QCW], F32, atm)
    psS2 = pst("psS2", [128, QCW], F32, atm)
    psO = pst("psO", [65, QCW], F32, atm)
    PTs = (PT0, PT1, PT2, PT3)
    psSs = (psS0, psS1, psS2)

    st8m = [0] * NH

    def load_strip(hd2, wqk):
        stp2 = strip0 if (hd2 % 2 == 0) else strip1

        @sec("gpsimd")
        def _(e, hd2=hd2, stp2=stp2, w=wqk):
            if w:
                e.wait_ge(pT.h, w)
            dma(e, "gpsimd", stp2[:], ap(strips, hd2 * 128 * STRIPW, [[STRIPW, 128], [1, STRIPW]]))
        s_st = qG.v

        @sec("vector")
        def _(e, stp2=stp2, w=s_st):
            e.wait_ge(qG.h, w)
            mark(pV, e.tensor_scalar(stp2[:], stp2[:], 8.0, 0.0, ALU.mult, ALU.add))
        st8m[hd2] = pV.v

    load_strip(0, 0)
    load_strip(1, 0)
    for hd in range(NH):
        stp = strip0 if (hd % 2 == 0) else strip1
        s_st8 = st8m[hd]

        dt, rb0 = hd // 2, 64 * (hd % 2)
        for qc in range(4):
            if qc == 0:
                phase_barrier()
            s_guard = pS.v
            s_guard2 = [pS.v]
            expm = []
            avm = []
            qkm = []

            def emit_av(kt):
                @sec("tensor")
                def _(e, hd=hd, kt=kt, w=expm[kt // 2], vg=s_guard2[0]):
                    e.wait_ge(pS.h, w)
                    if kt == 0:
                        e.wait_ge(pS.h, vg)
                    PTb = PTs[kt % 4]
                    lhs = Vb[:, kt * VROW + hd * 65: kt * VROW + (hd + 1) * 65]
                    for half in range(2):
                        mm = e.matmul(psO[:, half * 512:(half + 1) * 512], lhs,
                                      PTb[:, half * 512:(half + 1) * 512],
                                      start=(kt == 0), stop=(kt == 31))
                    mark(pT, mm)
                avm.append(pT.v)

            for kp in range(16):
                for kt in (2 * kp, 2 * kp + 1):
                    psSS = psSs[kt % 3]

                    @sec("tensor")
                    def _(e, qc=qc, kt=kt, psSS=psSS, dt=dt, rb0=rb0, stp=stp,
                          w=s_st8, sg=s_guard):
                        if kt < 3:
                            e.wait_ge(pV.h, w)
                            e.wait_ge(pS.h, sg)
                        else:
                            e.wait_ge(pS.h, expm[(kt - 3) // 2])
                        lhs = KT[rb0:rb0 + 64, dt * T + kt * 128: dt * T + (kt + 1) * 128]
                        soff = 4095 - kt * 128 + qc * QCW
                        for half in range(2):
                            q0 = dt * T + qc * QCW + half * 512
                            e.matmul(psSS[:, half * 512:(half + 1) * 512], lhs,
                                     QT[rb0:rb0 + 64, q0:q0 + 512], start=True, stop=False)
                            mm = e.matmul(psSS[:, half * 512:(half + 1) * 512], eyeb[:],
                                          stp[:, soff + half * 512: soff + half * 512 + 512],
                                          start=False, stop=True)
                        mark(pQ, mm)
                    qkm.append(pQ.v)

                if kp >= 1:
                    emit_av(2 * kp - 2)
                    emit_av(2 * kp - 1)

                @sec("scalar")
                def _(e, kp=kp, w=qkm[2 * kp + 1],
                      wpt=(avm[2 * kp - 3] if kp >= 2 else 0)):
                    e.wait_ge(pQ.h, w)
                    if wpt:
                        e.wait_ge(pT.h, wpt)
                    e.activation(PTs[(2 * kp) % 4][:], psSs[(2 * kp) % 3][:],
                                 AF.Exp, scale=0.125)
                    mark(pS, e.activation(PTs[(2 * kp + 1) % 4][:],
                                          psSs[(2 * kp + 1) % 3][:],
                                          AF.Exp, scale=0.125))
                expm.append(pS.v)
            emit_av(30)
            emit_av(31)
            s_av = pT.v

            @sec("scalar")
            def _(e, hd=hd, qc=qc, dt=dt, rb0=rb0, w=s_av, wq=qC.v):
                e.wait_ge(pT.h, w)
                e.wait_ge(qC.h, wq)
                e.copy(outN[rb0:rb0 + 64, dt * T + qc * QCW: dt * T + (qc + 1) * QCW],
                       psO[0:64, :])
                mark(pS, e.copy(den_sm[0:1, :], psO[64:65, :]))
            s_den = pS.v

            @sec("scalar")
            def _(e, hd=hd, qc=qc, w=s_den):
                e.wait_ge(pS.h, w)
                dma(e, "scalar", ap(den_dram, hd * T + qc * QCW, [[QCW, 1], [1, QCW]]),
                    den_sm[0:1, :])

        if hd + 2 < NH:
            load_strip(hd + 2, pT.v)

    if "dbg_den" in dbg:
        @sec("gpsimd")
        def _(e, w=qC.v):
            e.wait_ge(qC.h, w)
            dma(e, "gpsimd", dbg["dbg_den"][:, :], den_dram[:, :])
    for hd in range(NH):
        dt, rb0 = hd // 2, 64 * (hd % 2)

        @sec("scalar")
        def _(e, hd=hd, w=qC.v, wv=pV.v):
            e.wait_ge(qC.h, w)
            e.wait_ge(pV.h, wv)
            dma(e, "scalar", rbc[:, 0:T], ap(den_dram, hd * T, [[0, 128], [1, T]]))
        s_rb = qC.v

        @sec("vector")
        def _(e, hd=hd, dt=dt, rb0=rb0, w=s_rb):
            e.wait_ge(qC.h, w)
            mark(pV, e.reciprocal(rbc[:, 0:T], rbc[:, 0:T]))
            e.wait_ge(pV.h, pV.v)
            sl = outN[rb0:rb0 + 64, dt * T:(dt + 1) * T]
            mark(pV, e.tensor_tensor(sl, sl, rbc[rb0:rb0 + 64, 0:T], ALU.mult))
    s_norm = pV.v
    if "dbg_outn" in dbg:
        @sec("gpsimd")
        def _(e, w=s_norm):
            e.wait_ge(pV.h, w)
            dma(e, "gpsimd", dbg["dbg_outn"][:, :], outN[:])

    atm.close()
    phase_barrier()

    p3 = ExitStack()
    wotb = sbt("wotb", [128, 4 * C], BF16, p3)
    s2t = sbt("s2t", [128, CH], F32, p3)
    psW2 = pst("psW2", [128, CH], F32, p3)

    @sec("gpsimd")
    def _(e):
        dma(e, "gpsimd", wotb[:].rearrange("p (a m) -> p a m", a=4),
            wot.rearrange("(a p) m -> p a m", p=128))
    g_oe = qG.v

    for tq in range(8):
        phase_barrier()
        for m in range(8):
            @sec("tensor")
            def _(e, tq=tq, m=m, w=g_oe, wn=s_norm, wv=pV.v):
                if tq == 0 and m == 0:
                    e.wait_ge(qG.h, w)
                    e.wait_ge(pV.h, wn)
                e.wait_ge(pV.h, wv)
                for dt2 in range(4):
                    mm = e.matmul(psW2[:], wotb[:, dt2 * C + m * 128: dt2 * C + (m + 1) * 128],
                                  outN[:, dt2 * T + tq * CH: dt2 * T + (tq + 1) * CH],
                                  start=(dt2 == 0), stop=(dt2 == 3))
                mark(pT, mm)
            s_w2 = pT.v

            @sec("vector")
            def _(e, tq=tq, m=m, w=s_w2, wq=qS.v):
                e.wait_ge(pT.h, w)
                e.wait_ge(qS.h, wq)
                mark(pV, e.tensor_copy(s2t[:], psW2[:]))
            s_s2e = pV.v

            @sec("sync")
            def _(e, tq=tq, m=m, w=s_s2e):
                e.wait_ge(pV.h, w)
                hb, qloc = tq // 4, (tq % 4) * CH
                dma(e, "sync",
                    ap(s2rs_in, hb * (8 * 128 * TH) + m * 128 * TH + qloc,
                       [[TH, 128], [1, CH]]),
                    s2t[:])
    s_s2w = qS.v

    @sec("gpsimd")
    def _(e, w=s_s2w):
        e.wait_ge(qS.h, w)
        e.collective_compute(
            "ReduceScatter", ALU.add, replica_groups=[[0, 1], [2, 3], [4, 5], [6, 7]],
            ins=[s2rs_in.opt()], outs=[s2rs_out.opt()],
        ).then_inc(ccc.h, 1)
    ccc.v += 1
    cc_s2 = ccc.v
    if "dbg_s2" in dbg:
        @sec("gpsimd")
        def _(e, w=cc_s2):
            e.wait_ge(ccc.h, w)
            dma(e, "gpsimd", dbg["dbg_s2"][:, :].rearrange("p (c t) -> p c t", c=8),
                ap(s2rs_out, 0, [[TH, 128], [128 * TH, 8], [1, TCH]]))
    p3.close()
    at.close()
    phase_barrier()

    ff = ExitStack()
    make_ln(ff, "f", TCH, xsq=None, bf=True, single=True)
    wgqT = sbt("wgqT", [128, 8 * IFF], BF16, ff)
    wuqT = sbt("wuqT", [128, 8 * IFF], BF16, ff)
    wdqT = sbt("wdqT", [128, 22 * C], BF16, ff)
    ffw = ExitStack()
    wtmp = sbt("wtmp", [128, IFF], BF16, ffw)

    def load_tq(base_off, n_tiles, fdim, rank_tiles, dst, dstw):
        for it in range(n_tiles):
            rank, lt = it // rank_tiles, it % rank_tiles
            soff = rank * WSTR + base_off + lt * 128 * fdim

            @sec("gpsimd")
            def _(e, soff=soff, fdim=fdim, w=cc_wq, wt=pT.v):
                e.wait_ge(ccc.h, w)
                e.wait_ge(pT.h, wt)
                dma(e, "gpsimd", wtmp[:, 0:fdim], ap(wq_out, soff, [[fdim, 128], [1, fdim]]))
            s_wl = qG.v
            nsub = fdim // 128
            for ct in range(nsub):
                psm = LN["psAb"] if (ct % 2 == 0) else LN["psBb"]
                evb = pV.v

                @sec("tensor")
                def _(e, ct=ct, psm=psm, w=s_wl, evb=evb):
                    e.wait_ge(qG.h, w)
                    if ct >= 1:
                        e.wait_ge(pV.h, evb)
                    mark(pT, e.transpose(psm[0:128, 0:128], wtmp[:, ct * 128:(ct + 1) * 128], eyeb[:]))
                s_tr = pT.v

                @sec("vector")
                def _(e, it=it, ct=ct, psm=psm, dst=dst, dstw=dstw, w=s_tr):
                    e.wait_ge(pT.h, w)
                    mark(pV, e.tensor_copy(dst[:, ct * dstw + it * 128: ct * dstw + it * 128 + 128],
                                           psm[0:128, 0:128]))

    load_tq(0, 22, C, 6, wgqT, IFF)
    load_tq(WGR * C, 22, C, 6, wuqT, IFF)
    load_tq(2 * WGR * C, 8, IFF, 2, wdqT, C)
    ffw.close()
    phase_barrier()
    if "dbg_wg" in dbg:
        @sec("gpsimd")
        def _(e):
            dma(e, "gpsimd", dbg["dbg_wg"][:, :], wgqT[:])
            dma(e, "gpsimd", dbg["dbg_wu"][:, :], wuqT[:])
            dma(e, "gpsimd", dbg["dbg_wd"][:, :], wdqT[:])
        phase_barrier()

    x1c = sbt("x1c", [128, 8 * TCH], F32, ff)
    s2c = sbt("s2c", [128, 8 * TCH], F32, ff)
    x1nc = sbt("x1nc", [128, 8 * TCH], F32, ff)
    xtk = sbt("xtk", [128, 2 * C], F32, ff)
    xqk = sbt("xqk", [128, 2 * C], BF16, ff)
    xqTb = sbt("xqTb", [128, 8 * TCH], BF16, ff)
    sgi = sbt("sgi", [128, CH], F32, ff)
    hstg = sbt("hstg", [128, CH], BF16, ff)
    hqTb = sbt("hqTb", [128, 22 * TCH], BF16, ff)
    qt2f = sbt("qt2f", [128, C], F32, ff)
    x2t = qt2f
    upsb = xtk
    qbmf = sbt("qbmf", [128, C // 16], F32, ff)
    qrsf = sbt("qrsf", [128, C // 16], F32, ff)
    QF = {"v": LN["xsq"], "qa": x1nc, "t2": qt2f, "bm": qbmf, "rs": qrsf,
          "pitch": C // 16}
    psBIG = pst("psBIG", [128, 2048], F32, ff)

    ICW = [512, 512, 512, 512, 512, 256]
    for tc in range(8):
        phase_barrier()
        cl = tc * TCH
        sn_v = pV.v
        sn_q = qS.v

        @sec("gpsimd")
        def _(e, cl=cl, wv=sn_v, wc=cc_s2):
            e.wait_ge(pV.h, wv)
            e.wait_ge(ccc.h, wc)
            dma(e, "gpsimd", x1c[:].rearrange("p (c t) -> p c t", c=8),
                ap(x1_dram, cl, [[8 * TH, 128], [TH, 8], [1, TCH]]))
            dma(e, "gpsimd", s2c[:].rearrange("p (c t) -> p c t", c=8),
                ap(s2rs_out, cl, [[TH, 128], [128 * TH, 8], [1, TCH]]))
        s_x1l = qG.v

        @sec("vector")
        def _(e, w=s_x1l, wq=sn_q):
            e.wait_ge(qG.h, w)
            e.wait_ge(qS.h, wq)
        if "dbg_x1c" in dbg and tc == 0:
            @sec("gpsimd")
            def _(e, w=s_x1l):
                e.wait_ge(qG.h, w)
                dma(e, "gpsimd", dbg["dbg_x1c"][:, :], x1c[:])
        emit_ln(x1c, TCH, 0, x1nc, TCH, TCH)
        if "dbg_x1nc" in dbg and tc == 0:
            s_lnd = pV.v

            @sec("gpsimd")
            def _(e, w=s_lnd):
                e.wait_ge(pV.h, w)
                dma(e, "gpsimd", dbg["dbg_x1nc"][:, :], x1nc[:])
        emit_tr([(x1nc, ct * TCH + tt * 128, 8 * TCH, xtk, tt * C + ct * 128, 2 * C)
                 for tt in range(2) for ct in range(8)], fp32=True)
        s_xtk = pV.v
        if "dbg_xtk" in dbg and tc == 0:
            @sec("gpsimd")
            def _(e, w=s_xtk):
                e.wait_ge(pV.h, w)
                dma(e, "gpsimd", dbg["dbg_xtk"][:, :], xtk[:])

        @sec("vector")
        def _(e, w=s_xtk):
            e.wait_ge(pV.h, w)
            emit_quant(e, QF, xtk[:, 0:C], C, xqk[:, 0:C])
            mark(pV, emit_quant(e, QF, xtk[:, C:2 * C], C, xqk[:, C:2 * C]))
        s_xq = pV.v
        if "dbg_xq" in dbg and tc == 0:
            @sec("gpsimd")
            def _(e, w=s_xq):
                e.wait_ge(pV.h, w)
                dma(e, "gpsimd", dbg["dbg_xq"][:, :], xqk[:])

        emit_tr([(xqk, tt * C + ct * 128, 2 * C, xqTb, ct * TCH + tt * 128, 8 * TCH)
                 for tt in range(2) for ct in range(8)], fp32=False)
        s_xqT = pV.v

        for tt in range(2):
            for ic in range(6):
                icw = ICW[ic]
                io = ic * 512
                sg_prev = pS.v

                @sec("tensor")
                def _(e, tt=tt, io=io, icw=icw, w=s_xqT, wv=pV.v, ws=pS.v):
                    e.wait_ge(pV.h, max(w, wv))
                    e.wait_ge(pS.h, ws)
                    for ct in range(8):
                        e.matmul(psBIG[:, 0:icw], xqTb[:, ct * TCH + tt * 128: ct * TCH + tt * 128 + 128],
                                 wgqT[:, ct * IFF + io: ct * IFF + io + icw],
                                 start=(ct == 0), stop=(ct == 7))
                    for ct in range(8):
                        mm = e.matmul(psBIG[:, 512:512 + icw],
                                      xqTb[:, ct * TCH + tt * 128: ct * TCH + tt * 128 + 128],
                                      wuqT[:, ct * IFF + io: ct * IFF + io + icw],
                                      start=(ct == 0), stop=(ct == 7))
                    mark(pT, mm)
                s_gu = pT.v

                @sec("scalar")
                def _(e, icw=icw, w=s_gu):
                    e.wait_ge(pT.h, w)
                    e.activation(sgi[:, 0:icw], psBIG[:, 0:icw], AF.Silu)
                    mark(pS, e.copy(upsb[:, 0:icw], psBIG[:, 512:512 + icw]))
                s_sg2 = pS.v

                @sec("vector")
                def _(e, tt=tt, io=io, icw=icw, w=s_sg2, wt=pT.v):
                    e.wait_ge(pS.h, w)
                    e.wait_ge(pT.h, wt)
                    e.tensor_tensor(sgi[:, 0:icw], sgi[:, 0:icw], upsb[:, 0:icw], ALU.mult)
                    mark(pV, emit_quant(e, QF, sgi[:, 0:icw], icw, hstg[:, 0:icw]))
                s_hq = pV.v
                if "dbg_h" in dbg and tc == 0:
                    @sec("gpsimd")
                    def _(e, tt=tt, io=io, icw=icw, w=s_hq):
                        e.wait_ge(pV.h, w)
                        dma(e, "gpsimd", dbg["dbg_h"][:, tt * IFF + io: tt * IFF + io + icw],
                            sgi[:, 0:icw])
                        dma(e, "gpsimd", dbg["dbg_hq"][:, tt * IFF + io: tt * IFF + io + icw],
                            hstg[:, 0:icw])

                nsub = icw // 128
                emit_tr([(hstg, sub * 128, CH, hqTb, (io // 128 + sub) * TCH + tt * 128, 22 * TCH)
                         for sub in range(nsub)], fp32=False)
        s_hqT = pV.v

        for m in range(8):
            @sec("tensor")
            def _(e, m=m, wh=s_hqT, wv=pV.v, ws=pS.v):
                if m == 0:
                    e.wait_ge(pV.h, max(wh, wv))
                    e.wait_ge(pS.h, ws)
                for it in range(22):
                    mm = e.matmul(psBIG[:, m * TCH:(m + 1) * TCH],
                                  wdqT[:, it * C + m * 128: it * C + (m + 1) * 128],
                                  hqTb[:, it * TCH:(it + 1) * TCH],
                                  start=(it == 0), stop=(it == 21))
                mark(pT, mm)
        s_dn = pT.v

        for m in range(8):
            @sec("scalar")
            def _(e, m=m, w=s_dn, wv=pV.v, wg=qG.v):
                e.wait_ge(pT.h, w)
                e.wait_ge(pV.h, wv)
                if "dbg_dn" in dbg and tc == 0:
                    e.wait_ge(qG.h, wg)
                mark(pS, e.activation(sgi[:, 0:TCH], psBIG[:, m * TCH:(m + 1) * TCH], AF.Silu))
            s_g2 = pS.v
            if "dbg_dn" in dbg and tc == 0:
                @sec("gpsimd")
                def _(e, m=m, w=s_g2):
                    e.wait_ge(pS.h, w)
                    dma(e, "gpsimd", dbg["dbg_dn"][:, m * TCH:(m + 1) * TCH], sgi[:, 0:TCH])

            @sec("vector")
            def _(e, m=m, w=s_g2):
                e.wait_ge(pS.h, w)
                e.tensor_tensor(x2t[:, 0:TCH], s2c[:, m * TCH:(m + 1) * TCH], sgi[:, 0:TCH], ALU.mult)
                mark(pV, e.tensor_tensor(x2t[:, 0:TCH], x2t[:, 0:TCH],
                                         x1c[:, m * TCH:(m + 1) * TCH], ALU.add))
            s_x2 = pV.v
            if "dbg_x2" in dbg and tc == 0:
                @sec("gpsimd")
                def _(e, m=m, w=s_x2):
                    e.wait_ge(pV.h, w)
                    dma(e, "gpsimd", ap(dbg["dbg_x2"], m * TCH, [[8 * TCH, 128], [1, TCH]]), x2t[:, 0:TCH])
            emit_tr([(x2t, tt * 128, C, xtk, tt * C + m * 128, 2 * C) for tt in range(2)],
                    fp32=True)
        s_ot = pV.v

        @sec("sync")
        def _(e, cl=cl, w=s_ot):
            e.wait_ge(pV.h, w)
            dma(e, "sync", y_out[cl:cl + TCH, :].rearrange("(a p) c -> p a c", p=128),
                xtk[:, 0:2 * C].rearrange("p (a c) -> p a c", a=2))

    ff.close()
    phase_barrier()

    @sec("sync")
    def _(e, w=qS.v):
        e.wait_ge(qS.h, w)

    es.close()
    return nc


def _bucket(rel):
    import math
    try:
        import jax
        import jax.numpy as jnp
        cpu = jax.devices("cpu")[0]
    except Exception:
        jax = None
    if jax is not None:
        with jax.default_device(cpu):
            relative_position = jnp.asarray(rel, dtype=jnp.int32)
            num_buckets = 32
            ret = jnp.zeros_like(relative_position)
            n = -relative_position
            num_buckets //= 2
            ret = ret + (n < 0).astype(jnp.int32) * num_buckets
            n = jnp.abs(n)
            max_exact = num_buckets // 2
            is_small = n < max_exact
            val_if_large = max_exact + (
                jnp.log(jnp.maximum(n, 1).astype(jnp.float32) / max_exact)
                / math.log(128 / max_exact) * (num_buckets - max_exact)).astype(jnp.int32)
            val_if_large = jnp.minimum(val_if_large, num_buckets - 1)
            return np.asarray(ret + jnp.where(is_small, n, val_if_large))
    if True:
        num_buckets = 16
        ret = (rel > 0).astype(np.int64) * num_buckets
        n = np.abs(rel)
        max_exact = num_buckets // 2
        is_small = n < max_exact
        val_if_large = max_exact + (
            np.log(np.maximum(n, 1).astype(np.float32) / max_exact)
            / np.float32(math.log(128 / max_exact)) * (num_buckets - max_exact)).astype(np.int64)
        val_if_large = np.minimum(val_if_large, num_buckets - 1)
        return ret + np.where(is_small, n, val_if_large)


def _prep_core_inputs(core, I):
    b, h = core // 2, core % 2
    t0 = h * TH
    x = np.asarray(I["L0_attn_x"][b], dtype=np.float32)
    d = {}
    d["x_full"] = np.ascontiguousarray(x)
    prev = np.zeros((1, C), np.float32) if t0 == 0 else x[t0 - 1:t0]
    d["x_half"] = np.ascontiguousarray(np.concatenate(
        [prev, x[t0:t0 + TH], np.zeros((17 * 128 - TH - 1, C), np.float32)], axis=0))
    d["eye_f"] = np.eye(128, dtype=np.float32)
    wx = np.asarray(I["L0_attn_Wx"], np.float32)
    nsl = slice(8 * h, 8 * h + 8)
    d["wxt"] = np.ascontiguousarray(np.concatenate(
        [wx[32:, :], wx[0:16, :][nsl], wx[16:32, :][nsl]], axis=0).T)
    d["a_log"] = np.ascontiguousarray(np.asarray(I["L0_attn_A_log"], np.float32)[nsl][None, :])
    d["d_vec"] = np.ascontiguousarray(np.asarray(I["L0_attn_D"], np.float32).reshape(8, 128).T)
    d["mur"] = np.ascontiguousarray(np.asarray(I["L0_ffn_mu_r"], np.float32).reshape(8, 128).T)
    d["muk"] = np.ascontiguousarray(np.asarray(I["L0_ffn_mu_k"], np.float32).reshape(8, 128).T)
    d["wrt"] = np.ascontiguousarray(np.asarray(I["L0_ffn_W_r"], np.float32).T)
    d["wkt"] = np.ascontiguousarray(np.asarray(I["L0_ffn_W_k"], np.float32).T)
    d["wvt"] = np.ascontiguousarray(np.asarray(I["L0_ffn_W_v"], np.float32).T)
    d["woutt"] = np.ascontiguousarray(np.asarray(I["L0_attn_Wout"], np.float32).T)
    hsl = slice(8 * h * DK, 8 * h * DK + 512)
    d["wqt"] = np.ascontiguousarray(np.asarray(I["L1_attn_w_q"], np.float32)[hsl].T)
    d["wktA"] = np.ascontiguousarray(np.asarray(I["L1_attn_w_k"], np.float32)[hsl].T)
    d["wvtA"] = np.ascontiguousarray(np.asarray(I["L1_attn_w_v"], np.float32)[hsl].T)
    d["wot"] = np.ascontiguousarray(np.asarray(I["L1_attn_w_o"], np.float32).T[hsl])
    tab = np.asarray(I["L1_attn_rel_bias_table"], np.float32)
    p = np.arange(128)[:, None]
    m = np.arange(STRIPW)[None, :]
    buck = _bucket(p - m + 4095)
    st = np.empty((NH, 128, STRIPW), np.float32)
    for hl in range(NH):
        st[hl] = tab[buck, 8 * h + hl]
    d["strips"] = np.ascontiguousarray(st)
    wg = np.asarray(I["L1_ffn_w_gate"], np.float32)
    wu = np.asarray(I["L1_ffn_w_up"], np.float32)
    wd = np.asarray(I["L1_ffn_w_down"], np.float32)
    g = b
    wg_sl = np.zeros((WGR, C), np.float32)
    wu_sl = np.zeros((WGR, C), np.float32)
    r0 = WGR * g
    nr = max(0, min(WGR, IFF - r0))
    if nr > 0:
        wg_sl[:nr] = wg[r0:r0 + nr]
        wu_sl[:nr] = wu[r0:r0 + nr]
    d["wg_sl"] = wg_sl
    d["wu_sl"] = wu_sl
    d["wd_sl"] = np.ascontiguousarray(wd[WDR * g:WDR * g + WDR])
    return d


def kernel(**inputs):
    nc = build_nc()
    in_maps = [_prep_core_inputs(c, inputs) for c in range(8)]
    res = run_bass_kernel_spmd(nc, in_maps, core_ids=list(range(8)))
    out = np.empty((B, T, C), np.float32)
    for c in range(8):
        b, h = c // 2, c % 2
        out[b, h * TH:(h + 1) * TH] = res.results[c]["y_out"]
    return out


# version 08 (final)
# speedup vs baseline: 1.0358x; 1.0358x over previous
import numpy as np
from contextlib import ExitStack

import concourse.bass as bass
import concourse.mybir as mybir
from concourse.bass_utils import run_bass_kernel_spmd
from concourse.alu_op_type import AluOpType as ALU
from concourse.aot_env import allow_ici_all_to_all

F32 = mybir.dt.float32
BF16 = mybir.dt.bfloat16
AF = mybir.ActivationFunctionType
AX = mybir.AxisListType

B, T, C = 4, 4096, 1024
TH = 2048
NH, DK = 8, 64
IFF = 2816
CH = 512
NCH = T // CH
STRIPW = 8320
QCW = 1024
TCH = 256
VROW = NH * 65 + 8
WGR = 768
WDR = 256
WSTR = 2 * WGR * C + WDR * IFF
DBG = ()


def build_nc():
    nc = bass.Bass("TRN2", target_bir_lowering=False, detect_race_conditions=False)
    es = ExitStack()

    def din(name, shape, dtype=F32):
        return nc.dram_tensor(name, shape, dtype, kind="ExternalInput").ap()

    x_full = din("x_full", [T, C])
    x_half = din("x_half", [17 * 128, C])
    eye_f = din("eye_f", [128, 128])
    wxt = din("wxt", [C, 1040])
    a_log = din("a_log", [1, 8])
    d_vec = din("d_vec", [128, 8])
    mur = din("mur", [128, 8])
    muk = din("muk", [128, 8])
    wrt = din("wrt", [C, C])
    wkt = din("wkt", [C, C])
    wvt = din("wvt", [C, C])
    woutt = din("woutt", [C, C])
    wqt = din("wqt", [C, 512])
    wktA = din("wktA", [C, 512])
    wvtA = din("wvtA", [C, 512])
    wot = din("wot", [512, C])
    strips = din("strips", [NH, 128, STRIPW])
    wg_sl = din("wg_sl", [WGR, C])
    wu_sl = din("wu_sl", [WGR, C])
    wd_sl = din("wd_sl", [WDR, IFF])

    y_out = nc.dram_tensor("y_out", [TH, C], F32, kind="ExternalOutput").ap()

    dbg = {}
    for name, shape in [
        ("dbg_xn0", [128, 8 * CH]), ("dbg_delta0", [128, 8 * CH]),
        ("dbg_ypart", [128, 8 * CH]), ("dbg_g1", [128, 8 * CH]),
        ("dbg_x1", [128, 8 * CH]), ("dbg_s2", [128, 8 * TCH]),
        ("dbg_xq", [128, 2 * C]), ("dbg_x2", [128, 8 * TCH]),
        ("dbg_x1c", [128, 8 * TCH]), ("dbg_x1nc", [128, 8 * TCH]),
        ("dbg_xtk", [128, 2 * C]), ("dbg_outn", [128, 4 * T]),
        ("dbg_qt2", [128, 4 * T]), ("dbg_kt2", [128, 4 * T]),
        ("dbg_x1f", [128, 8 * T]), ("dbg_agin", [128, 8 * TH]),
        ("dbg_x1b", [128, 8 * CH]),
        ("dbg_den", [NH, T]),
        ("dbg_y", [128, 8 * CH]),
        ("dbg_wg", [128, 8 * IFF]), ("dbg_wu", [128, 8 * IFF]),
        ("dbg_wd", [128, 22 * C]), ("dbg_h", [128, 2 * IFF]),
        ("dbg_hq", [128, 2 * IFF]), ("dbg_dn", [128, 8 * TCH]),
    ]:
        if name in DBG:
            dbg[name] = nc.dram_tensor(name, shape, F32, kind="ExternalOutput").ap()

    def dsc(name, shape, dtype=F32):
        return nc.dram_tensor(name, shape, dtype).ap()

    bc_dram = dsc("bc_dram", [16, T])
    ab_dram = dsc("ab_dram", [2, CH])
    rs_in = dsc("rs_in", [2 * 8 * 128 * TH])
    rs_out = dsc("rs_out", [8 * 128 * TH])
    g1_dram = dsc("g1_dram", [128, 8 * TH])
    x1_dram = dsc("x1_dram", [128, 8 * TH])
    ag_in = dsc("ag_in", [8 * 128 * TH], BF16)
    ag_out = dsc("ag_out", [2 * 8 * 128 * TH], BF16)
    den_dram = dsc("den_dram", [NH, T])
    s2rs_in = dsc("s2rs_in", [2 * 8 * 128 * TH])
    s2rs_out = dsc("s2rs_out", [8 * 128 * TH])
    wq_in = dsc("wq_in", [WSTR], BF16)
    wq_out = dsc("wq_out", [4 * WSTR], BF16)

    def sbt(name, shape, dtype=F32, stack=None):
        return (stack or es).enter_context(nc.sbuf_tensor(name, shape, dtype))

    def pst(name, shape, dtype=F32, stack=None):
        return (stack or es).enter_context(nc.psum_tensor(name, shape, dtype))

    eyef = sbt("eyef", [128, 128])
    eyeb = sbt("eyeb", [128, 128], BF16)
    ones_f = sbt("ones_f", [128, 1])
    c_eps = sbt("c_eps", [1, 1])
    c_one = sbt("c_one", [128, 1])
    a_ap = sbt("a_ap", [128, 8])
    d_half = sbt("d_half", [128, 8])
    mur_t = sbt("mur_t", [128, 8])
    muk_t = sbt("muk_t", [128, 8])

    class Ctr:
        def __init__(self, name):
            self.h = es.enter_context(nc.semaphore(name))
            self.v = 0
    qS, qG, qC, qV = Ctr("qS"), Ctr("qG"), Ctr("qC"), Ctr("qV")
    pV, pS, pT, pP, pQ = Ctr("pV"), Ctr("pS"), Ctr("pT"), Ctr("pP"), Ctr("pQ")
    ccc = Ctr("ccc")
    QOF = {"sync": qS, "gpsimd": qG, "scalar": qC, "vector": qV}

    block = es.enter_context(nc.Block())

    def sec(eng):
        return getattr(block, eng)

    def dma(e, eng, out, in_):
        q = QOF[eng]
        if q.v > 0:
            e.wait_ge(q.h, q.v)
        e.dma_start(out=out, in_=in_).then_inc(q.h, 16)
        q.v += 16

    def mark(ctr, instr):
        instr.then_inc(ctr.h, 1)
        ctr.v += 1
        return ctr.v

    def ap(t, off, dims):
        if hasattr(t, "tensor"):
            return bass.AP(t.tensor, t.offset + off, dims)
        return bass.AP(t, off, dims)

    def snap():
        return {c: c.v for c in (qS, qG, qC, qV, pV, pS, pT, pP, pQ, ccc)}

    def wait(e, sn):
        for c, v in sn.items():
            if v > 0:
                e.wait_ge(c.h, v)

    def phase_barrier():
        sn = snap()
        for eng in ("sync", "scalar", "vector", "tensor", "gpsimd"):
            @sec(eng)
            def _(e):
                wait(e, sn)

    @sec("sync")
    def _(e):
        dma(e, "sync", eyef[:], eye_f[:])
        dma(e, "sync", d_half[:], d_vec[:])
        dma(e, "sync", mur_t[:], mur[:])
        dma(e, "sync", muk_t[:], muk[:])
        dma(e, "sync", a_ap[:, 0:8], ap(a_log, 0, [[0, 128], [1, 8]]))
    s_ld = snap()

    @sec("gpsimd")
    def _(e):
        dma(e, "gpsimd", eyeb[:], eye_f[:])

    @sec("scalar")
    def _(e):
        wait(e, s_ld)
        mark(pS, e.activation(a_ap[:, 0:8], a_ap[:, 0:8], AF.Exp))
    s_aexp = pS.v

    @sec("vector")
    def _(e):
        e.memset(ones_f[:], 1.0)
        e.memset(c_eps[:], 1e-5)
        e.memset(c_one[:], 1.0)
        e.wait_ge(pS.h, s_aexp)
        e.tensor_scalar(a_ap[:, 0:8], a_ap[:, 0:8], -1.0, 0.0, ALU.mult, ALU.add)
        wait(e, s_ld)
        mark(pV, e.tensor_scalar(d_half[:], d_half[:], 0.5, 0.0, ALU.mult, ALU.add))

    LN = {}

    def make_ln(stack, tag, W, xsq=None, bf=False, single=False):
        LN["W"] = W
        LN["psA"] = pst("psA" + tag, [128, 128], F32, stack)
        LN["psB"] = LN["psA"] if single else pst("psB" + tag, [128, 128], F32, stack)
        if bf:
            LN["psAb"] = pst("psAb" + tag, [128, 128], BF16, stack)
            LN["psBb"] = LN["psAb"] if single else pst("psBb" + tag, [128, 128], BF16, stack)
        LN["psln"] = pst("psln" + tag, [1, 2 * W], F32, stack)
        LN["stats"] = sbt("stats" + tag, [1, 6 * W], F32, stack)
        LN["absc"] = sbt("absc" + tag, [128, W], F32, stack)
        LN["bbsc"] = sbt("bbsc" + tag, [128, W], F32, stack)
        LN["xsq"] = xsq if xsq is not None else sbt("xsq" + tag, [128, 8 * W], F32, stack)

    def emit_tr(jobs, fp32=True):
        idd = eyef if fp32 else eyeb
        kA, kB = ("psA", "psB") if fp32 else ("psAb", "psBb")
        sn = snap()
        evbase = pV.v
        trbase = pT.v
        single = LN[kA] is LN[kB]
        for k, (st_, so, sp, dt_, do, dp) in enumerate(jobs):
            psm = LN[kA] if (k % 2 == 0) else LN[kB]

            @sec("tensor")
            def _(e, k=k, psm=psm, st_=st_, so=so, sp=sp):
                if k == 0:
                    wait(e, sn)
                if single and k >= 1:
                    e.wait_ge(pV.h, evbase + k)
                elif k >= 2:
                    e.wait_ge(pV.h, evbase + k - 1)
                mark(pT, e.transpose(psm[0:128, 0:128],
                                     ap(st_, so, [[sp, 128], [1, 128]]), idd[:]))

            @sec("vector")
            def _(e, k=k, psm=psm, dt_=dt_, do=do, dp=dp):
                if k == 0:
                    wait(e, sn)
                e.wait_ge(pT.h, trbase + k + 1)
                mark(pV, e.tensor_copy(ap(dt_, do, [[dp, 128], [1, 128]]), psm[0:128, 0:128]))

    def emit_ln(xt_tile, xt_w, col0, xn_tile, xn_w, W):
        sn = snap()
        S = LN["W"]
        xq = LN["xsq"]

        @sec("scalar")
        def _(e):
            wait(e, sn)
            mark(pS, e.activation(
                ap(xq, 0, [[8 * S, 128], [S, 8], [1, W]]),
                ap(xt_tile, col0, [[xt_w * 8, 128], [xt_w, 8], [1, W]]),
                AF.Square))
        s_sq = pS.v

        @sec("tensor")
        def _(e):
            wait(e, sn)
            for ct in range(8):
                e.matmul(LN["psln"][0:1, 0:W], ones_f[:, 0:1],
                         ap(xt_tile, col0 + ct * xt_w, [[xt_w * 8, 128], [1, W]]),
                         start=(ct == 0), stop=(ct == 7))
            e.wait_ge(pS.h, s_sq)
            for ct in range(8):
                mm = e.matmul(LN["psln"][0:1, S:S + W], ones_f[:, 0:1],
                              ap(xq, ct * S, [[8 * S, 128], [1, W]]),
                              start=(ct == 0), stop=(ct == 7))
            mark(pT, mm)
        s_mm = pT.v

        @sec("scalar")
        def _(e):
            st = LN["stats"]
            e.wait_ge(pT.h, s_mm)
            e.copy(st[0:1, 0:W], LN["psln"][0:1, 0:W])
            mark(pS, e.copy(st[0:1, S:S + W], LN["psln"][0:1, S:S + W]))
        s_pse = pS.v

        @sec("vector")
        def _(e):
            st = LN["stats"]
            e.wait_ge(pS.h, s_pse)
            e.tensor_scalar(st[0:1, 0:W], st[0:1, 0:W], 1.0 / C, 0.0, ALU.mult, ALU.add)
            e.tensor_scalar(st[0:1, S:S + W], st[0:1, S:S + W], 1.0 / C, 0.0, ALU.mult, ALU.add)
            e.tensor_tensor(st[0:1, 2 * S:2 * S + W], st[0:1, 0:W], st[0:1, 0:W], ALU.mult)
            mark(pV, e.tensor_tensor(st[0:1, 3 * S:3 * S + W], st[0:1, S:S + W],
                                     st[0:1, 2 * S:2 * S + W], ALU.subtract))
        s_var = pV.v

        @sec("scalar")
        def _(e):
            st = LN["stats"]
            e.wait_ge(pV.h, s_var)
            e.activation(st[0:1, 4 * S:4 * S + W], st[0:1, 3 * S:3 * S + W], AF.Ln, bias=c_eps[:])
            mark(pS, e.activation(st[0:1, 4 * S:4 * S + W], st[0:1, 4 * S:4 * S + W], AF.Exp, scale=-0.5))
        s_rsv = pS.v

        @sec("vector")
        def _(e):
            st = LN["stats"]
            e.wait_ge(pS.h, s_rsv)
            mark(pV, e.tensor_tensor(st[0:1, 5 * S:5 * S + W], st[0:1, 0:W],
                                     st[0:1, 4 * S:4 * S + W], ALU.mult))
        s_b2 = pV.v

        @sec("scalar")
        def _(e):
            st = LN["stats"]
            e.wait_ge(pV.h, s_b2)
            dma(e, "scalar", ab_dram[0:1, 0:W], st[0:1, 4 * S:4 * S + W])
            dma(e, "scalar", ab_dram[1:2, 0:W], st[0:1, 5 * S:5 * S + W])
            e.wait_ge(qC.h, qC.v)
            dma(e, "scalar", LN["absc"][:, 0:W], ap(ab_dram, 0, [[0, 128], [1, W]]))
            dma(e, "scalar", LN["bbsc"][:, 0:W], ap(ab_dram, CH, [[0, 128], [1, W]]))
        s_abr = qC.v

        @sec("vector")
        def _(e):
            e.wait_ge(qC.h, s_abr)
            for ct in range(8):
                src = ap(xt_tile, col0 + ct * xt_w, [[xt_w * 8, 128], [1, W]])
                dst = ap(xn_tile, col0 + ct * xn_w, [[xn_w * 8, 128], [1, W]])
                e.tensor_tensor(dst, src, LN["absc"][:, 0:W], ALU.mult)
                mm = e.tensor_tensor(dst, dst, LN["bbsc"][:, 0:W], ALU.subtract)
            mark(pV, mm)

    def make_q(stack, F, tag):
        return {
            "v": sbt("qv" + tag, [128, F], F32, stack),
            "qa": sbt("qqa" + tag, [128, F], F32, stack),
            "t2": sbt("qt2" + tag, [128, F], F32, stack),
            "bm": sbt("qbm" + tag, [128, F // 16], F32, stack),
            "rs": sbt("qrs" + tag, [128, F // 16], F32, stack),
            "pitch": F // 16,
        }

    def emit_quant(e, q_, xin, F, out_bf):
        nb = F // 16
        pw = q_["pitch"]
        mark(pV, e.tensor_reduce(q_["bm"][:, 0:nb], xin.rearrange("p (b s) -> p b s", s=16),
                                 AX.X, ALU.max, apply_absolute_value=True))
        e.wait_ge(pV.h, pV.v)
        mark(pV, e.tensor_scalar(q_["bm"][:, 0:nb], q_["bm"][:, 0:nb], 1.0 / 6.0, 1e-12,
                                 ALU.mult, ALU.max))
        e.wait_ge(pV.h, pV.v)
        mark(pV, e.reciprocal(q_["rs"][:, 0:nb], q_["bm"][:, 0:nb]))
        e.wait_ge(pV.h, pV.v)
        rs_bc = ap(q_["rs"], 0, [[pw, 128], [1, nb], [0, 16]])
        sc_bc = ap(q_["bm"], 0, [[pw, 128], [1, nb], [0, 16]])
        e.tensor_tensor(q_["v"][:, 0:F].rearrange("p (b s) -> p b s", s=16),
                        xin.rearrange("p (b s) -> p b s", s=16), rs_bc, ALU.mult)
        e.tensor_scalar(q_["qa"][:, 0:F], q_["v"][:, 0:F], 1e30, 1.0, ALU.mult, ALU.min)
        e.tensor_scalar(q_["qa"][:, 0:F], q_["qa"][:, 0:F], -1.0, 0.0, ALU.max, ALU.add)
        e.tensor_tensor(q_["v"][:, 0:F], q_["v"][:, 0:F], q_["qa"][:, 0:F], ALU.mult)
        av = q_["v"][:, 0:F]
        e.tensor_scalar(q_["t2"][:, 0:F], av, 0.25, 1.0, ALU.is_gt, ALU.mult)
        for m in (0.75, 1.25, 1.75):
            e.scalar_tensor_tensor(q_["t2"][:, 0:F], av, m, q_["t2"][:, 0:F], ALU.is_gt, ALU.add)
        e.tensor_scalar(q_["t2"][:, 0:F], q_["t2"][:, 0:F], 0.5, 0.0, ALU.mult, ALU.add)
        for m in (2.5, 3.5, 5.0, 5.0):
            e.scalar_tensor_tensor(q_["t2"][:, 0:F], av, m, q_["t2"][:, 0:F], ALU.is_gt, ALU.add)
        e.tensor_tensor(q_["t2"][:, 0:F], q_["t2"][:, 0:F], q_["qa"][:, 0:F], ALU.mult)
        return e.tensor_tensor(out_bf.rearrange("p (b s) -> p b s", s=16),
                               q_["t2"][:, 0:F].rearrange("p (b s) -> p b s", s=16),
                               sc_bc, ALU.mult)

    qs = ExitStack()
    q_in = sbt("q_in", [128, IFF], F32, qs)
    q_outt = sbt("q_outt", [128, IFF], BF16, qs)
    QW = make_q(qs, IFF, "w")

    wjobs = []
    off = 0
    for (mat, rows, fdim) in [(wg_sl, WGR, C), (wu_sl, WGR, C), (wd_sl, WDR, IFF)]:
        r = 0
        while r < rows:
            rr = min(128, rows - r)
            wjobs.append((mat, r, rr, fdim, off))
            off += rr * fdim
            r += rr
    assert off == WSTR

    for (mat, r, rr, fdim, woff) in wjobs:
        @sec("gpsimd")
        def _(e, mat=mat, r=r, rr=rr, fdim=fdim, wv=pV.v, wq2=qG.v):
            e.wait_ge(pV.h, wv)
            e.wait_ge(qG.h, wq2)
            dma(e, "gpsimd", q_in[0:rr, 0:fdim], mat[r:r + rr, :])
        gq = qG.v

        @sec("vector")
        def _(e, rr=rr, fdim=fdim, gq=gq):
            e.wait_ge(qG.h, gq)
            mark(pV, emit_quant(e, QW, q_in[0:rr, 0:fdim], fdim, q_outt[0:rr, 0:fdim]))
        vq = pV.v

        @sec("gpsimd")
        def _(e, rr=rr, fdim=fdim, woff=woff, vq=vq):
            e.wait_ge(pV.h, vq)
            dma(e, "gpsimd", ap(wq_in, woff, [[fdim, rr], [1, fdim]]), q_outt[0:rr, 0:fdim])

    @sec("gpsimd")
    def _(e, gw=qG.v):
        e.wait_ge(qG.h, gw)
        e.collective_compute(
            "AllGather", ALU.bypass, replica_groups=[[0, 2, 4, 6], [1, 3, 5, 7]],
            ins=[wq_in.opt()], outs=[wq_out.opt()],
        ).then_inc(ccc.h, 1)
    ccc.v += 1
    cc_wq = ccc.v
    qs.close()

    lns = ExitStack()
    make_ln(lns, "0", CH)
    l0 = ExitStack()
    xtok = sbt("xtok", [128, 4 * C], F32, l0)
    xT = sbt("xT", [128, 8 * CH], F32, l0)
    xnT = sbt("xnT", [128, 8 * CH], F32, l0)
    xnb = sbt("xnb", [128, 8 * CH], BF16, l0)
    deltach = sbt("deltach", [128, 8 * CH], F32, l0)
    dxch = xtok
    bcsb = sbt("bcsb", [16, CH], F32, l0)
    bbc = sbt("bbc", [128, 8 * CH], F32, l0)
    cbc = sbt("cbc", [128, 8 * CH], F32, l0)
    dAex0 = sbt("dAex0", [128, 8 * CH], F32, l0)
    dAex1 = sbt("dAex1", [128, 8 * CH], F32, l0)
    uu = sbt("uu", [128, CH], F32, l0)
    hh = sbt("hh", [128, CH], F32, l0)
    acc0 = sbt("acc0", [128, CH], F32, l0)
    acc1 = sbt("acc1", [128, CH], F32, l0)
    yt0 = sbt("yt0", [128, CH], F32, l0)
    yt1 = sbt("yt1", [128, CH], F32, l0)
    carry = sbt("carry", [128, 64], F32, l0)
    wxtb = sbt("wxtb", [128, 8 * 1040], BF16, l0)
    psxp0 = pst("psxp0", [128, CH], F32, l0)
    psxp1 = pst("psxp1", [128, CH], F32, l0)
    psbc = pst("psbc", [16, CH], F32, l0)

    @sec("gpsimd")
    def _(e):
        dma(e, "gpsimd", wxtb[:].rearrange("p (a m) -> p a m", a=8),
            wxt.rearrange("(a p) m -> p a m", p=128))
    g_wx = qG.v

    @sec("vector")
    def _(e):
        mark(pV, e.memset(carry[:], 0.0))

    for ch in range(NCH):
        phase_barrier()
        t0 = ch * CH
        sn_pT = pT.v
        qs_guard = qS.v

        @sec("sync")
        def _(e, t0=t0, w=sn_pT):
            e.wait_ge(pT.h, w)
            dma(e, "sync", xtok[:].rearrange("p (a c) -> p a c", a=4),
                x_full[t0:t0 + CH, :].rearrange("(a p) c -> p a c", p=128))

        @sec("vector")
        def _(e, w=qs_guard):
            e.wait_ge(qS.h, w)
        emit_tr([(xtok, tt * C + ct * 128, 4 * C, xT, ct * CH + tt * 128, 8 * CH)
                 for tt in range(4) for ct in range(8)], fp32=True)
        emit_ln(xT, CH, 0, xnT, CH, CH)
        sn_xn = pV.v

        @sec("vector")
        def _(e, w=sn_xn):
            e.wait_ge(pV.h, w)
            mark(pV, e.tensor_copy(xnb[:], xnT[:]))
        s_xnb = pV.v

        if "dbg_xn0" in dbg and ch == 0:
            @sec("gpsimd")
            def _(e, w=s_xnb):
                e.wait_ge(pV.h, w)
                dma(e, "gpsimd", dbg["dbg_xn0"][:, :], xnT[:])

        sp_marks = []
        bc_ev = [0]
        for m in range(9):
            psm = psxp0 if (m % 2 == 0) else psxp1

            @sec("tensor")
            def _(e, m=m, psm=psm, wxn=s_xnb, wld=g_wx):
                if m == 0:
                    e.wait_ge(qG.h, wld)
                    e.wait_ge(pV.h, wxn)
                if m >= 2:
                    e.wait_ge(pS.h, sp_marks[m - 2]) if m - 2 < len(sp_marks) else None
                for ct in range(8):
                    if m < 8:
                        mm = e.matmul(psm[:], wxtb[:, ct * 1040 + m * 128: ct * 1040 + (m + 1) * 128],
                                      xnb[:, ct * CH:(ct + 1) * CH], start=(ct == 0), stop=(ct == 7))
                    else:
                        mm = e.matmul(psbc[:], wxtb[:, ct * 1040 + 1024: ct * 1040 + 1040],
                                      xnb[:, ct * CH:(ct + 1) * CH], start=(ct == 0), stop=(ct == 7))
                mark(pT, mm)
            s_mm = pT.v

            if m < 8:
                @sec("scalar")
                def _(e, m=m, psm=psm, w=s_mm):
                    e.wait_ge(pT.h, w)
                    mark(pS, e.activation(deltach[:, m * CH:(m + 1) * CH], psm[:], AF.Exp))
                sp_marks.append(pS.v)
            else:
                @sec("scalar")
                def _(e, w=s_mm):
                    e.wait_ge(pT.h, w)
                    mark(pS, e.copy(bcsb[:], psbc[:]))
                bc_ev[0] = pS.v

        @sec("scalar")
        def _(e, w=sp_marks[-1]):
            e.wait_ge(pS.h, w)
            mark(pS, e.activation(deltach[:], deltach[:], AF.Ln, bias=c_one[:]))
        s_sp = pS.v

        if "dbg_delta0" in dbg and ch == 0:
            @sec("gpsimd")
            def _(e, w=s_sp):
                e.wait_ge(pS.h, w)
                dma(e, "gpsimd", dbg["dbg_delta0"][:, :], deltach[:])

        @sec("scalar")
        def _(e, w=bc_ev[0], t0=t0):
            e.wait_ge(pS.h, w)
            dma(e, "scalar", bc_dram[:, t0:t0 + CH], bcsb[:])
        s_bcw = qC.v

        @sec("scalar")
        def _(e, w=s_bcw, t0=t0, wv=pV.v):
            e.wait_ge(qC.h, w)
            e.wait_ge(pV.h, wv)
            dma(e, "scalar", bbc[:].rearrange("p (n t) -> p n t", n=8),
                ap(bc_dram, t0, [[0, 128], [T, 8], [1, CH]]))
            dma(e, "scalar", cbc[:].rearrange("p (n t) -> p n t", n=8),
                ap(bc_dram, 8 * T + t0, [[0, 128], [T, 8], [1, CH]]))
        s_bcr = qC.v

        @sec("vector")
        def _(e, w=s_sp):
            e.wait_ge(pS.h, w)
            mark(pV, e.tensor_tensor(dxch[:], deltach[:], xnT[:], ALU.mult))
        s_dx = pV.v

        scan_done = []
        da_ready = []
        ydma = []
        hb, cloc = ch // 4, (ch % 4) * CH
        for ct in range(8):
            sl = slice(ct * CH, (ct + 1) * CH)
            dAx = dAex0 if (ct % 2 == 0) else dAex1
            yt = yt0 if (ct % 2 == 0) else yt1
            vguard = scan_done[ct - 2] if ct >= 2 else 0

            @sec("gpsimd")
            def _(e, ct=ct, dAx=dAx, w=s_sp, vg=vguard):
                e.wait_ge(pS.h, w)
                if vg:
                    e.wait_ge(pV.h, vg)
                mark(pP, e.tensor_tensor(
                    ap(dAx, 0, [[8 * CH, 128], [CH, 8], [1, CH]]),
                    ap(deltach, ct * CH, [[8 * CH, 128], [0, 8], [1, CH]]),
                    ap(a_ap, 0, [[8, 128], [1, 8], [0, CH]]), ALU.mult))
            s_pm = pP.v

            @sec("scalar")
            def _(e, dAx=dAx, w=s_pm):
                e.wait_ge(pP.h, w)
                mark(pS, e.activation(dAx[:], dAx[:], AF.Exp))
            da_ready.append(pS.v)

            @sec("vector")
            def _(e, ct=ct, dAx=dAx, yt=yt, sl=sl, wda=da_ready[ct], wbc=s_bcr,
                  wdx=s_dx, wdm=(ydma[ct - 2] if ct >= 2 else 0)):
                e.wait_ge(pS.h, wda)
                e.wait_ge(qC.h, wbc)
                e.wait_ge(pV.h, wdx)
                if wdm:
                    e.wait_ge(qS.h, wdm)
                for n in range(8):
                    e.tensor_tensor(uu[:], dxch[:, sl], bbc[:, n * CH:(n + 1) * CH], ALU.mult)
                    e.tensor_tensor_scan(hh[:], dAx[:, n * CH:(n + 1) * CH], uu[:],
                                         carry[:, ct * 8 + n: ct * 8 + n + 1],
                                         ALU.mult, ALU.add)
                    if n == 0:
                        e.tensor_tensor(acc0[:], hh[:], cbc[:, 0:CH], ALU.mult)
                    else:
                        src, dst = (acc0, acc1) if (n % 2 == 1) else (acc1, acc0)
                        e.tensor_tensor(uu[:], hh[:], cbc[:, n * CH:(n + 1) * CH], ALU.mult)
                        e.tensor_tensor(dst[:], src[:], uu[:], ALU.add)
                    e.tensor_copy(carry[:, ct * 8 + n: ct * 8 + n + 1], hh[:, CH - 1:CH])
                mark(pV, e.scalar_tensor_tensor(yt[:], xnT[:, sl],
                                                d_half[:, ct:ct + 1], acc1[:],
                                                ALU.mult, ALU.add))
            scan_done.append(pV.v)

            @sec("sync")
            def _(e, ct=ct, yt=yt, hb=hb, cloc=cloc, w=scan_done[ct]):
                e.wait_ge(pV.h, w)
                dma(e, "sync", ap(rs_in, hb * (8 * 128 * TH) + cloc + ct * 128 * TH,
                                  [[TH, 128], [1, CH]]), yt[:])
            ydma.append(qS.v)
        s_y = pV.v

    l0.close()
    phase_barrier()

    rk = ExitStack()
    xtok2 = sbt("xtok2", [128, 4 * C], F32, rk)
    xTb = sbt("xTb", [128, 8 * CH], F32, rk)
    xnbuf0 = sbt("xnbuf0", [128, 8 * CH], F32, rk)
    xnbuf1 = sbt("xnbuf1", [128, 8 * CH], F32, rk)
    mixt = sbt("mixt", [128, CH], F32, rk)
    xrb = sbt("xrb", [128, 8 * CH], BF16, rk)
    xkb = sbt("xkb", [128, 8 * CH], BF16, rk)
    rT = sbt("rT", [128, 8 * CH], F32, rk)
    kq = sbt("kq", [128, 8 * CH], F32, rk)
    kqb = sbt("kqb", [128, 8 * CH], BF16, rk)
    gtm = sbt("gtm", [128, CH], F32, rk)
    vsb = sbt("vsb", [128, CH], F32, rk)
    wrtb = sbt("wrtb", [128, 8 * C], BF16, rk)
    wktb = sbt("wktb", [128, 8 * C], BF16, rk)
    wvtb = sbt("wvtb", [128, 8 * C], BF16, rk)
    psR0 = pst("psR0", [128, CH], F32, rk)
    psR1 = pst("psR1", [128, CH], F32, rk)

    @sec("gpsimd")
    def _(e):
        for (dst, src) in [(wrtb, wrt), (wktb, wkt), (wvtb, wvt)]:
            dma(e, "gpsimd", dst[:].rearrange("p (a m) -> p a m", a=8),
                src.rearrange("(a p) m -> p a m", p=128))
    g_wr = qG.v

    def lnblock(blk):
        phase_barrier()
        buf = xnbuf0 if (blk % 2 == 0) else xnbuf1
        r0 = blk * CH
        nrows = max(0, min(CH, 17 * 128 - r0))
        ntt = (nrows + 127) // 128
        sn_pT = pT.v

        @sec("sync")
        def _(e, r0=r0, ntt=ntt, w=sn_pT):
            e.wait_ge(pT.h, w)
            if ntt > 0:
                dma(e, "sync", xtok2[:, 0:ntt * C].rearrange("p (a c) -> p a c", a=ntt),
                    x_half[r0:r0 + ntt * 128, :].rearrange("(a p) c -> p a c", p=128))
        if ntt < 4:
            sn0 = snap()

            @sec("vector")
            def _(e, sn0=sn0, ntt=ntt):
                wait(e, sn0)
                mark(pV, e.memset(xtok2[:, ntt * C:4 * C], 0.0))
        emit_tr([(xtok2, tt * C + ct * 128, 4 * C, xTb, ct * CH + tt * 128, 8 * CH)
                 for tt in range(4) for ct in range(8)], fp32=True)
        emit_ln(xTb, CH, 0, buf, CH, CH)
        return buf

    b_prev = lnblock(0)
    for tc in range(4):
        b_cur = b_prev
        b_next = lnblock(tc + 1)
        phase_barrier()
        sn0 = snap()

        @sec("vector")
        def _(e, b_cur=b_cur, b_next=b_next, sn0=sn0):
            wait(e, sn0)
            for ct in range(8):
                cs = ct * CH
                e.tensor_tensor(mixt[:, 0:CH - 1], b_cur[:, cs + 1:cs + CH],
                                b_cur[:, cs:cs + CH - 1], ALU.subtract)
                e.tensor_tensor(mixt[:, CH - 1:CH], b_next[:, cs:cs + 1],
                                b_cur[:, cs + CH - 1:cs + CH], ALU.subtract)
                e.scalar_tensor_tensor(xrb[:, cs:cs + CH], mixt[:, 0:CH],
                                       mur_t[:, ct:ct + 1], b_cur[:, cs:cs + CH],
                                       ALU.mult, ALU.add)
                mm = e.scalar_tensor_tensor(xkb[:, cs:cs + CH], mixt[:, 0:CH],
                                            muk_t[:, ct:ct + 1], b_cur[:, cs:cs + CH],
                                            ALU.mult, ALU.add)
            mark(pV, mm)
        s_mix = pV.v

        for m in range(8):
            psm = psR0 if (m % 2 == 0) else psR1

            @sec("tensor")
            def _(e, m=m, psm=psm, w=s_mix, wld=g_wr, ws=pS.v):
                if m == 0:
                    e.wait_ge(pV.h, w)
                    e.wait_ge(qG.h, wld)
                else:
                    e.wait_ge(pS.h, ws)
                for ct in range(8):
                    mm = e.matmul(psm[:], wrtb[:, ct * C + m * 128: ct * C + (m + 1) * 128],
                                  xrb[:, ct * CH:(ct + 1) * CH], start=(ct == 0), stop=(ct == 7))
                mark(pT, mm)
            s_rm = pT.v

            @sec("scalar")
            def _(e, m=m, psm=psm, w=s_rm):
                e.wait_ge(pT.h, w)
                mark(pS, e.activation(rT[:, m * CH:(m + 1) * CH], psm[:], AF.Sigmoid))
        s_rsig = pS.v
        for m in range(8):
            psm = psR0 if (m % 2 == 0) else psR1

            @sec("tensor")
            def _(e, m=m, psm=psm, ws=pS.v, w0=s_rsig):
                if m < 2:
                    e.wait_ge(pS.h, w0)
                else:
                    e.wait_ge(pS.h, ws)
                for ct in range(8):
                    mm = e.matmul(psm[:], wktb[:, ct * C + m * 128: ct * C + (m + 1) * 128],
                                  xkb[:, ct * CH:(ct + 1) * CH], start=(ct == 0), stop=(ct == 7))
                mark(pT, mm)
            s_km = pT.v

            @sec("scalar")
            def _(e, m=m, psm=psm, w=s_km):
                e.wait_ge(pT.h, w)
                mark(pS, e.activation(kq[:, m * CH:(m + 1) * CH], psm[:], AF.Relu))
        s_kr = pS.v

        @sec("scalar")
        def _(e, w=s_kr):
            e.wait_ge(pS.h, w)
            mark(pS, e.activation(kq[:], kq[:], AF.Square))
        s_ksq = pS.v

        @sec("vector")
        def _(e, w=s_ksq):
            e.wait_ge(pS.h, w)
            mark(pV, e.tensor_copy(kqb[:], kq[:]))
        s_kqb = pV.v

        for m in range(8):
            psm = psR0 if (m % 2 == 0) else psR1

            @sec("tensor")
            def _(e, m=m, psm=psm, w=s_kqb, ws=pS.v):
                if m < 2:
                    e.wait_ge(pV.h, w)
                else:
                    e.wait_ge(pS.h, ws)
                for ct in range(8):
                    mm = e.matmul(psm[:], wvtb[:, ct * C + m * 128: ct * C + (m + 1) * 128],
                                  kqb[:, ct * CH:(ct + 1) * CH], start=(ct == 0), stop=(ct == 7))
                mark(pT, mm)
            s_vm = pT.v

            @sec("scalar")
            def _(e, m=m, psm=psm, w=s_vm):
                e.wait_ge(pT.h, w)
                mark(pS, e.copy(vsb[:], psm[:]))
            s_vev = pS.v

            @sec("vector")
            def _(e, m=m, w=s_vev, wq=qS.v):
                e.wait_ge(pS.h, w)
                e.wait_ge(qS.h, wq)
                mark(pV, e.tensor_tensor(gtm[:], rT[:, m * CH:(m + 1) * CH], vsb[:], ALU.mult))
            s_rv = pV.v

            @sec("scalar")
            def _(e, w=s_rv):
                e.wait_ge(pV.h, w)
                mark(pS, e.activation(gtm[:], gtm[:], AF.Silu))
            s_g = pS.v

            @sec("sync")
            def _(e, tc=tc, m=m, w=s_g):
                e.wait_ge(pS.h, w)
                dma(e, "sync", ap(g1_dram, m * TH + tc * CH, [[8 * TH, 128], [1, CH]]), gtm[:])
            if "dbg_g1" in dbg and tc == 0:
                @sec("gpsimd")
                def _(e, m=m, w=s_g):
                    e.wait_ge(pS.h, w)
                    dma(e, "gpsimd", ap(dbg["dbg_g1"], m * CH, [[8 * CH, 128], [1, CH]]), gtm[:])
        b_prev = b_next

    rk.close()
    phase_barrier()

    p2 = ExitStack()
    yb = sbt("yb", [128, 8 * CH], BF16, p2)
    g1c = sbt("g1c", [128, 8 * CH], F32, p2)
    xTh = sbt("xTh", [128, 8 * CH], F32, p2)
    x1T = sbt("x1T", [128, 8 * CH], F32, p2)
    x1n = sbt("x1n", [128, 8 * CH], F32, p2)
    x1nb = sbt("x1nb", [128, 8 * CH], BF16, p2)
    tmp2 = sbt("tmp2", [128, CH], F32, p2)
    woutb = sbt("woutb", [128, 8 * C], BF16, p2)
    xtok3 = sbt("xtok3", [128, 4 * C], F32, p2)
    s1sb = sbt("s1sb", [128, CH], F32, p2)
    psW = pst("psW", [128, CH], F32, p2)

    @sec("gpsimd")
    def _(e, w=qS.v):
        e.wait_ge(qS.h, w)
        e.collective_compute(
            "ReduceScatter", ALU.add, replica_groups=[[0, 1], [2, 3], [4, 5], [6, 7]],
            ins=[rs_in.opt()], outs=[rs_out.opt()],
        ).then_inc(ccc.h, 1)
        dma(e, "gpsimd", woutb[:].rearrange("p (a m) -> p a m", a=8),
            woutt.rearrange("(a p) m -> p a m", p=128))
    ccc.v += 1
    cc_y = ccc.v
    g_wo = qG.v

    if "dbg_y" in dbg:
        @sec("gpsimd")
        def _(e, w=cc_y):
            e.wait_ge(ccc.h, w)
            dma(e, "gpsimd", dbg["dbg_y"][:, :].rearrange("p (c t) -> p c t", c=8),
                ap(rs_out, 0, [[TH, 128], [128 * TH, 8], [1, CH]]))

    for tc in range(4):
        phase_barrier()
        cl = tc * CH
        sn_v = pV.v

        @sec("gpsimd")
        def _(e, cl=cl, w=cc_y, wv=sn_v):
            e.wait_ge(ccc.h, w)
            e.wait_ge(pV.h, wv)
            dma(e, "gpsimd", yb[:].rearrange("p (c t) -> p c t", c=8),
                ap(rs_out, cl, [[TH, 128], [128 * TH, 8], [1, CH]]))
            dma(e, "gpsimd", g1c[:].rearrange("p (c t) -> p c t", c=8),
                ap(g1_dram, cl, [[8 * TH, 128], [TH, 8], [1, CH]]))
        s_ybl = qG.v
        sn_pT = pT.v

        @sec("sync")
        def _(e, cl=cl, w=sn_pT):
            e.wait_ge(pT.h, w)
            dma(e, "sync", xtok3[:].rearrange("p (a c) -> p a c", a=4),
                x_half[1 + cl:1 + cl + CH, :].rearrange("(a p) c -> p a c", p=128))
        emit_tr([(xtok3, tt * C + ct * 128, 4 * C, xTh, ct * CH + tt * 128, 8 * CH)
                 for tt in range(4) for ct in range(8)], fp32=True)
        s_xTh = pV.v

        for m in range(8):
            @sec("tensor")
            def _(e, m=m, w=s_ybl, wld=g_wo, ws=pS.v):
                if m == 0:
                    e.wait_ge(qG.h, max(w, wld))
                e.wait_ge(pS.h, ws)
                for ct in range(8):
                    mm = e.matmul(psW[:], woutb[:, ct * C + m * 128: ct * C + (m + 1) * 128],
                                  yb[:, ct * CH:(ct + 1) * CH], start=(ct == 0), stop=(ct == 7))
                mark(pT, mm)
            s_wm = pT.v

            @sec("scalar")
            def _(e, m=m, w=s_wm, wv=pV.v):
                e.wait_ge(pT.h, w)
                e.wait_ge(pV.h, wv)
                mark(pS, e.copy(s1sb[:], psW[:]))
            s_ev = pS.v

            @sec("vector")
            def _(e, m=m, w=s_ev, wx=s_xTh, wq=qS.v):
                e.wait_ge(pS.h, w)
                e.wait_ge(pV.h, wx)
                if m == 0:
                    e.wait_ge(qS.h, wq)
                e.tensor_tensor(tmp2[:], s1sb[:], g1c[:, m * CH:(m + 1) * CH], ALU.mult)
                mark(pV, e.tensor_tensor(x1T[:, m * CH:(m + 1) * CH], tmp2[:],
                                         xTh[:, m * CH:(m + 1) * CH], ALU.add))
        s_x1 = pV.v

        @sec("sync")
        def _(e, cl=cl, w=s_x1):
            e.wait_ge(pV.h, w)
            dma(e, "sync", ap(x1_dram, cl, [[8 * TH, 128], [TH, 8], [1, CH]]),
                x1T[:].rearrange("p (c t) -> p c t", c=8))
        if "dbg_x1" in dbg and tc == 0:
            @sec("gpsimd")
            def _(e, w=s_x1):
                e.wait_ge(pV.h, w)
                dma(e, "gpsimd", dbg["dbg_x1"][:, :], x1T[:])

        emit_ln(x1T, CH, 0, x1n, CH, CH)
        s_ln1 = pV.v

        @sec("vector")
        def _(e, cl=cl, w=s_ln1, wq=qS.v):
            e.wait_ge(pV.h, w)
            e.wait_ge(qS.h, wq)
            mark(pV, e.tensor_copy(x1nb[:], x1n[:]))
        s_x1nb = pV.v

        @sec("sync")
        def _(e, cl=cl, w=s_x1nb):
            e.wait_ge(pV.h, w)
            dma(e, "sync", ap(ag_in, cl, [[TH, 128], [128 * TH, 8], [1, CH]]),
                x1nb[:].rearrange("p (c t) -> p c t", c=8))

    if "dbg_x1b" in dbg:
        @sec("gpsimd")
        def _(e, w=qS.v):
            e.wait_ge(qS.h, w)
            dma(e, "gpsimd", dbg["dbg_x1b"][:, :].rearrange("p (c t) -> p c t", c=8),
                ap(x1_dram, CH, [[8 * TH, 128], [TH, 8], [1, CH]]))
            e.wait_ge(qG.h, qG.v)

    if "dbg_agin" in dbg:
        @sec("gpsimd")
        def _(e, w=qS.v):
            e.wait_ge(qS.h, w)
            dma(e, "gpsimd", dbg["dbg_agin"][:, :].rearrange("p (c t) -> p c t", c=8),
                ap(ag_in, 0, [[TH, 128], [128 * TH, 8], [1, TH]]))
            e.wait_ge(qG.h, qG.v)

    @sec("gpsimd")
    def _(e, w=qS.v):
        e.wait_ge(qS.h, w)
        e.collective_compute(
            "AllGather", ALU.bypass, replica_groups=[[0, 1], [2, 3], [4, 5], [6, 7]],
            ins=[ag_in.opt()], outs=[ag_out.opt()],
        ).then_inc(ccc.h, 1)
    ccc.v += 1
    cc_x1n = ccc.v
    p2.close()
    lns.close()
    phase_barrier()

    at = ExitStack()
    QT = sbt("QT", [128, 4 * T], BF16, at)
    KT = sbt("KT", [128, 4 * T], BF16, at)
    Vb = sbt("Vb", [128, 32 * VROW], BF16, at)
    outN = sbt("outN", [128, 4 * T], BF16, at)

    atp = ExitStack()
    x1f = sbt("x1f", [128, 8 * T], BF16, atp)
    wAb = sbt("wAb", [128, 8 * 512], BF16, atp)
    psQ0 = pst("psQ0", [128, CH], F32, atp)
    psQ1 = pst("psQ1", [128, CH], F32, atp)

    @sec("gpsimd")
    def _(e, w=cc_x1n):
        dma(e, "gpsimd", wAb[:].rearrange("p (a m) -> p a m", a=8),
            wqt.rearrange("(a p) m -> p a m", p=128))
        e.wait_ge(ccc.h, w)
        for r in range(2):
            dma(e, "gpsimd", ap(x1f, r * TH, [[8 * T, 128], [T, 8], [1, TH]]),
                ap(ag_out, r * (8 * 128 * TH), [[TH, 128], [128 * TH, 8], [1, TH]]))
    g_at = qG.v

    for dst_t, wsrc in ((QT, wqt), (KT, wktA)):
        if dst_t is KT:
            @sec("gpsimd")
            def _(e, wsrc=wsrc, wt=pT.v):
                e.wait_ge(pT.h, wt)
                dma(e, "gpsimd", wAb[:].rearrange("p (a m) -> p a m", a=8),
                    wsrc.rearrange("(a p) m -> p a m", p=128))
        g_w = qG.v
        for m in range(4):
            for tn in range(8):
                @sec("tensor")
                def _(e, dst_t=dst_t, m=m, tn=tn, w=g_at, w2=g_w, ws=pS.v):
                    if m == 0 and tn == 0:
                        e.wait_ge(qG.h, max(w, w2))
                    e.wait_ge(pS.h, ws)
                    for ct in range(8):
                        mm = e.matmul(psQ0[:], wAb[:, ct * 512 + m * 128: ct * 512 + (m + 1) * 128],
                                      x1f[:, ct * T + tn * CH: ct * T + (tn + 1) * CH],
                                      start=(ct == 0), stop=(ct == 7))
                    mark(pT, mm)
                s_q = pT.v

                @sec("scalar")
                def _(e, dst_t=dst_t, m=m, tn=tn, w=s_q):
                    e.wait_ge(pT.h, w)
                    mark(pS, e.copy(dst_t[:, m * T + tn * CH: m * T + (tn + 1) * CH], psQ0[:]))

    @sec("gpsimd")
    def _(e, wt=pT.v):
        e.wait_ge(pT.h, wt)
        dma(e, "gpsimd", wAb[:].rearrange("p (a m) -> p a m", a=8),
            wvtA.rearrange("(a p) m -> p a m", p=128))
    g_wv = qG.v

    if "dbg_x1f" in dbg:
        @sec("gpsimd")
        def _(e, w=g_at):
            e.wait_ge(qG.h, w)
            dma(e, "gpsimd", dbg["dbg_x1f"][:, :], x1f[:])

    if "dbg_qt2" in dbg:
        @sec("gpsimd")
        def _(e, w=pV.v):
            e.wait_ge(pV.h, w)
            dma(e, "gpsimd", dbg["dbg_qt2"][:, :], QT[:])
            dma(e, "gpsimd", dbg["dbg_kt2"][:, :], KT[:])

    @sec("vector")
    def _(e):
        mark(pV, e.memset(Vb[:], 1.0))
    s_vmem = pV.v
    for tt in range(32):
        for hd in range(NH):
            @sec("tensor")
            def _(e, tt=tt, hd=hd, w=g_wv, ws=pS.v, wm=pV.v):
                if tt == 0 and hd == 0:
                    e.wait_ge(qG.h, w)
                    e.wait_ge(pS.h, ws)
                e.wait_ge(pV.h, wm)
                for ct in range(8):
                    mm = e.matmul(psQ0[:, 0:64],
                                  x1f[:, ct * T + tt * 128: ct * T + (tt + 1) * 128],
                                  wAb[:, ct * 512 + hd * 64: ct * 512 + (hd + 1) * 64],
                                  start=(ct == 0), stop=(ct == 7))
                mark(pT, mm)
            s_v = pT.v

            @sec("vector")
            def _(e, tt=tt, hd=hd, w=s_v):
                e.wait_ge(pT.h, w)
                mark(pV, e.tensor_copy(Vb[:, tt * VROW + hd * 65: tt * VROW + hd * 65 + 64],
                                       psQ0[:, 0:64]))
    atp.close()
    phase_barrier()

    atm = ExitStack()
    strip0 = sbt("strip0", [128, STRIPW], BF16, atm)
    strip1 = sbt("strip1", [128, STRIPW], BF16, atm)
    PT0 = sbt("PT0", [128, QCW], BF16, atm)
    PT1 = sbt("PT1", [128, QCW], BF16, atm)
    PT2 = sbt("PT2", [128, QCW], BF16, atm)
    PT3 = sbt("PT3", [128, QCW], BF16, atm)
    den_sm = sbt("den_sm", [1, QCW], F32, atm)
    rbc = sbt("rbc", [128, T], F32, atm)
    psS0 = pst("psS0", [128, QCW], F32, atm)
    psS1 = pst("psS1", [128, QCW], F32, atm)
    psS2 = pst("psS2", [128, QCW], F32, atm)
    psO = pst("psO", [65, QCW], F32, atm)
    PTs = (PT0, PT1, PT2, PT3)
    psSs = (psS0, psS1, psS2)

    st8m = [0] * NH

    def load_strip(hd2, wqk):
        stp2 = strip0 if (hd2 % 2 == 0) else strip1

        @sec("gpsimd")
        def _(e, hd2=hd2, stp2=stp2, w=wqk):
            if w:
                e.wait_ge(pT.h, w)
            dma(e, "gpsimd", stp2[:], ap(strips, hd2 * 128 * STRIPW, [[STRIPW, 128], [1, STRIPW]]))
        s_st = qG.v

        @sec("vector")
        def _(e, stp2=stp2, w=s_st):
            e.wait_ge(qG.h, w)
            mark(pV, e.tensor_scalar(stp2[:], stp2[:], 8.0, 0.0, ALU.mult, ALU.add))
        st8m[hd2] = pV.v

    load_strip(0, 0)
    load_strip(1, 0)
    for hd in range(NH):
        stp = strip0 if (hd % 2 == 0) else strip1
        s_st8 = st8m[hd]

        dt, rb0 = hd // 2, 64 * (hd % 2)
        for qc in range(4):
            if qc == 0:
                phase_barrier()
            s_guard = pS.v
            s_guard2 = [pS.v]
            expm = []
            avm = []
            qkm = []

            def emit_av(kt):
                @sec("tensor")
                def _(e, hd=hd, kt=kt, w=expm[kt // 2], vg=s_guard2[0]):
                    e.wait_ge(pS.h, w)
                    if kt == 0:
                        e.wait_ge(pS.h, vg)
                    PTb = PTs[kt % 4]
                    lhs = Vb[:, kt * VROW + hd * 65: kt * VROW + (hd + 1) * 65]
                    for half in range(2):
                        mm = e.matmul(psO[:, half * 512:(half + 1) * 512], lhs,
                                      PTb[:, half * 512:(half + 1) * 512],
                                      start=(kt == 0), stop=(kt == 31))
                    mark(pT, mm)
                avm.append(pT.v)

            for kp in range(16):
                for kt in (2 * kp, 2 * kp + 1):
                    psSS = psSs[kt % 3]

                    @sec("tensor")
                    def _(e, qc=qc, kt=kt, psSS=psSS, dt=dt, rb0=rb0, stp=stp,
                          w=s_st8, sg=s_guard):
                        if kt < 3:
                            e.wait_ge(pV.h, w)
                            e.wait_ge(pS.h, sg)
                        else:
                            e.wait_ge(pS.h, expm[(kt - 3) // 2])
                        lhs = KT[rb0:rb0 + 64, dt * T + kt * 128: dt * T + (kt + 1) * 128]
                        soff = 4095 - kt * 128 + qc * QCW
                        for half in range(2):
                            q0 = dt * T + qc * QCW + half * 512
                            e.matmul(psSS[:, half * 512:(half + 1) * 512], lhs,
                                     QT[rb0:rb0 + 64, q0:q0 + 512], start=True, stop=False)
                            mm = e.matmul(psSS[:, half * 512:(half + 1) * 512], eyeb[:],
                                          stp[:, soff + half * 512: soff + half * 512 + 512],
                                          start=False, stop=True)
                        mark(pQ, mm)
                    qkm.append(pQ.v)

                if kp >= 1:
                    emit_av(2 * kp - 2)
                    emit_av(2 * kp - 1)

                @sec("scalar")
                def _(e, kp=kp, w=qkm[2 * kp + 1],
                      wpt=(avm[2 * kp - 3] if kp >= 2 else 0)):
                    e.wait_ge(pQ.h, w)
                    if wpt:
                        e.wait_ge(pT.h, wpt)
                    e.activation(PTs[(2 * kp) % 4][:], psSs[(2 * kp) % 3][:],
                                 AF.Exp, scale=0.125)
                    mark(pS, e.activation(PTs[(2 * kp + 1) % 4][:],
                                          psSs[(2 * kp + 1) % 3][:],
                                          AF.Exp, scale=0.125))
                expm.append(pS.v)
            emit_av(30)
            emit_av(31)
            s_av = pT.v

            @sec("scalar")
            def _(e, hd=hd, qc=qc, dt=dt, rb0=rb0, w=s_av, wq=qC.v):
                e.wait_ge(pT.h, w)
                e.wait_ge(qC.h, wq)
                e.copy(outN[rb0:rb0 + 64, dt * T + qc * QCW: dt * T + (qc + 1) * QCW],
                       psO[0:64, :])
                mark(pS, e.copy(den_sm[0:1, :], psO[64:65, :]))
            s_den = pS.v

            @sec("scalar")
            def _(e, hd=hd, qc=qc, w=s_den):
                e.wait_ge(pS.h, w)
                dma(e, "scalar", ap(den_dram, hd * T + qc * QCW, [[QCW, 1], [1, QCW]]),
                    den_sm[0:1, :])

        if hd + 2 < NH:
            load_strip(hd + 2, pT.v)

    if "dbg_den" in dbg:
        @sec("gpsimd")
        def _(e, w=qC.v):
            e.wait_ge(qC.h, w)
            dma(e, "gpsimd", dbg["dbg_den"][:, :], den_dram[:, :])
    for hd in range(NH):
        dt, rb0 = hd // 2, 64 * (hd % 2)

        @sec("scalar")
        def _(e, hd=hd, w=qC.v, wv=pV.v):
            e.wait_ge(qC.h, w)
            e.wait_ge(pV.h, wv)
            dma(e, "scalar", rbc[:, 0:T], ap(den_dram, hd * T, [[0, 128], [1, T]]))
        s_rb = qC.v

        @sec("vector")
        def _(e, hd=hd, dt=dt, rb0=rb0, w=s_rb):
            e.wait_ge(qC.h, w)
            mark(pV, e.reciprocal(rbc[:, 0:T], rbc[:, 0:T]))
            e.wait_ge(pV.h, pV.v)
            sl = outN[rb0:rb0 + 64, dt * T:(dt + 1) * T]
            mark(pV, e.tensor_tensor(sl, sl, rbc[rb0:rb0 + 64, 0:T], ALU.mult))
    s_norm = pV.v
    if "dbg_outn" in dbg:
        @sec("gpsimd")
        def _(e, w=s_norm):
            e.wait_ge(pV.h, w)
            dma(e, "gpsimd", dbg["dbg_outn"][:, :], outN[:])

    atm.close()
    phase_barrier()

    p3 = ExitStack()
    wotb = sbt("wotb", [128, 4 * C], BF16, p3)
    s2t = sbt("s2t", [128, CH], F32, p3)
    psW2 = pst("psW2", [128, CH], F32, p3)

    @sec("gpsimd")
    def _(e):
        dma(e, "gpsimd", wotb[:].rearrange("p (a m) -> p a m", a=4),
            wot.rearrange("(a p) m -> p a m", p=128))
    g_oe = qG.v

    for tq in range(8):
        phase_barrier()
        for m in range(8):
            @sec("tensor")
            def _(e, tq=tq, m=m, w=g_oe, wn=s_norm, wv=pV.v):
                if tq == 0 and m == 0:
                    e.wait_ge(qG.h, w)
                    e.wait_ge(pV.h, wn)
                e.wait_ge(pV.h, wv)
                for dt2 in range(4):
                    mm = e.matmul(psW2[:], wotb[:, dt2 * C + m * 128: dt2 * C + (m + 1) * 128],
                                  outN[:, dt2 * T + tq * CH: dt2 * T + (tq + 1) * CH],
                                  start=(dt2 == 0), stop=(dt2 == 3))
                mark(pT, mm)
            s_w2 = pT.v

            @sec("vector")
            def _(e, tq=tq, m=m, w=s_w2, wq=qS.v):
                e.wait_ge(pT.h, w)
                e.wait_ge(qS.h, wq)
                mark(pV, e.tensor_copy(s2t[:], psW2[:]))
            s_s2e = pV.v

            @sec("sync")
            def _(e, tq=tq, m=m, w=s_s2e):
                e.wait_ge(pV.h, w)
                hb, qloc = tq // 4, (tq % 4) * CH
                dma(e, "sync",
                    ap(s2rs_in, hb * (8 * 128 * TH) + m * 128 * TH + qloc,
                       [[TH, 128], [1, CH]]),
                    s2t[:])
    s_s2w = qS.v

    @sec("gpsimd")
    def _(e, w=s_s2w):
        e.wait_ge(qS.h, w)
        e.collective_compute(
            "ReduceScatter", ALU.add, replica_groups=[[0, 1], [2, 3], [4, 5], [6, 7]],
            ins=[s2rs_in.opt()], outs=[s2rs_out.opt()],
        ).then_inc(ccc.h, 1)
    ccc.v += 1
    cc_s2 = ccc.v
    if "dbg_s2" in dbg:
        @sec("gpsimd")
        def _(e, w=cc_s2):
            e.wait_ge(ccc.h, w)
            dma(e, "gpsimd", dbg["dbg_s2"][:, :].rearrange("p (c t) -> p c t", c=8),
                ap(s2rs_out, 0, [[TH, 128], [128 * TH, 8], [1, TCH]]))
    p3.close()
    at.close()
    phase_barrier()

    ff = ExitStack()
    make_ln(ff, "f", TCH, xsq=None, bf=True, single=True)
    wgqT = sbt("wgqT", [128, 8 * IFF], BF16, ff)
    wuqT = sbt("wuqT", [128, 8 * IFF], BF16, ff)
    wdqT = sbt("wdqT", [128, 22 * C], BF16, ff)
    ffw = ExitStack()
    wtmp = sbt("wtmp", [128, IFF], BF16, ffw)

    def load_tq(base_off, n_tiles, fdim, rank_tiles, dst, dstw):
        for it in range(n_tiles):
            rank, lt = it // rank_tiles, it % rank_tiles
            soff = rank * WSTR + base_off + lt * 128 * fdim

            @sec("gpsimd")
            def _(e, soff=soff, fdim=fdim, w=cc_wq, wt=pT.v):
                e.wait_ge(ccc.h, w)
                e.wait_ge(pT.h, wt)
                dma(e, "gpsimd", wtmp[:, 0:fdim], ap(wq_out, soff, [[fdim, 128], [1, fdim]]))
            s_wl = qG.v
            nsub = fdim // 128
            for ct in range(nsub):
                psm = LN["psAb"] if (ct % 2 == 0) else LN["psBb"]
                evb = pV.v

                @sec("tensor")
                def _(e, ct=ct, psm=psm, w=s_wl, evb=evb):
                    e.wait_ge(qG.h, w)
                    if ct >= 1:
                        e.wait_ge(pV.h, evb)
                    mark(pT, e.transpose(psm[0:128, 0:128], wtmp[:, ct * 128:(ct + 1) * 128], eyeb[:]))
                s_tr = pT.v

                @sec("vector")
                def _(e, it=it, ct=ct, psm=psm, dst=dst, dstw=dstw, w=s_tr):
                    e.wait_ge(pT.h, w)
                    mark(pV, e.tensor_copy(dst[:, ct * dstw + it * 128: ct * dstw + it * 128 + 128],
                                           psm[0:128, 0:128]))

    load_tq(0, 22, C, 6, wgqT, IFF)
    load_tq(WGR * C, 22, C, 6, wuqT, IFF)
    load_tq(2 * WGR * C, 8, IFF, 2, wdqT, C)
    ffw.close()
    phase_barrier()
    if "dbg_wg" in dbg:
        @sec("gpsimd")
        def _(e):
            dma(e, "gpsimd", dbg["dbg_wg"][:, :], wgqT[:])
            dma(e, "gpsimd", dbg["dbg_wu"][:, :], wuqT[:])
            dma(e, "gpsimd", dbg["dbg_wd"][:, :], wdqT[:])
        phase_barrier()

    x1c = sbt("x1c", [128, 8 * TCH], F32, ff)
    s2c = sbt("s2c", [128, 8 * TCH], F32, ff)
    x1nc = sbt("x1nc", [128, 8 * TCH], F32, ff)
    xtk = sbt("xtk", [128, 2 * C], F32, ff)
    xqk = sbt("xqk", [128, 2 * C], BF16, ff)
    xqTb = sbt("xqTb", [128, 8 * TCH], BF16, ff)
    sgi = sbt("sgi", [128, CH], F32, ff)
    hstg = sbt("hstg", [128, CH], BF16, ff)
    hqTb = sbt("hqTb", [128, 22 * TCH], BF16, ff)
    qt2f = sbt("qt2f", [128, C], F32, ff)
    x2t = qt2f
    upsb = xtk
    qbmf = sbt("qbmf", [128, C // 16], F32, ff)
    qrsf = sbt("qrsf", [128, C // 16], F32, ff)
    QF = {"v": LN["xsq"], "qa": x1nc, "t2": qt2f, "bm": qbmf, "rs": qrsf,
          "pitch": C // 16}
    psBIG = pst("psBIG", [128, 2048], F32, ff)

    ICW = [512, 512, 512, 512, 512, 256]
    for tc in range(8):
        phase_barrier()
        cl = tc * TCH
        sn_v = pV.v
        sn_q = qS.v

        @sec("gpsimd")
        def _(e, cl=cl, wv=sn_v, wc=cc_s2):
            e.wait_ge(pV.h, wv)
            e.wait_ge(ccc.h, wc)
            dma(e, "gpsimd", x1c[:].rearrange("p (c t) -> p c t", c=8),
                ap(x1_dram, cl, [[8 * TH, 128], [TH, 8], [1, TCH]]))
            dma(e, "gpsimd", s2c[:].rearrange("p (c t) -> p c t", c=8),
                ap(s2rs_out, cl, [[TH, 128], [128 * TH, 8], [1, TCH]]))
        s_x1l = qG.v

        @sec("vector")
        def _(e, w=s_x1l, wq=sn_q):
            e.wait_ge(qG.h, w)
            e.wait_ge(qS.h, wq)
        if "dbg_x1c" in dbg and tc == 0:
            @sec("gpsimd")
            def _(e, w=s_x1l):
                e.wait_ge(qG.h, w)
                dma(e, "gpsimd", dbg["dbg_x1c"][:, :], x1c[:])
        emit_ln(x1c, TCH, 0, x1nc, TCH, TCH)
        if "dbg_x1nc" in dbg and tc == 0:
            s_lnd = pV.v

            @sec("gpsimd")
            def _(e, w=s_lnd):
                e.wait_ge(pV.h, w)
                dma(e, "gpsimd", dbg["dbg_x1nc"][:, :], x1nc[:])
        emit_tr([(x1nc, ct * TCH + tt * 128, 8 * TCH, xtk, tt * C + ct * 128, 2 * C)
                 for tt in range(2) for ct in range(8)], fp32=True)
        s_xtk = pV.v
        if "dbg_xtk" in dbg and tc == 0:
            @sec("gpsimd")
            def _(e, w=s_xtk):
                e.wait_ge(pV.h, w)
                dma(e, "gpsimd", dbg["dbg_xtk"][:, :], xtk[:])

        @sec("vector")
        def _(e, w=s_xtk):
            e.wait_ge(pV.h, w)
            emit_quant(e, QF, xtk[:, 0:C], C, xqk[:, 0:C])
            mark(pV, emit_quant(e, QF, xtk[:, C:2 * C], C, xqk[:, C:2 * C]))
        s_xq = pV.v
        if "dbg_xq" in dbg and tc == 0:
            @sec("gpsimd")
            def _(e, w=s_xq):
                e.wait_ge(pV.h, w)
                dma(e, "gpsimd", dbg["dbg_xq"][:, :], xqk[:])

        emit_tr([(xqk, tt * C + ct * 128, 2 * C, xqTb, ct * TCH + tt * 128, 8 * TCH)
                 for tt in range(2) for ct in range(8)], fp32=False)
        s_xqT = pV.v

        for tt in range(2):
            for ic in range(6):
                icw = ICW[ic]
                io = ic * 512
                sg_prev = pS.v

                @sec("tensor")
                def _(e, tt=tt, io=io, icw=icw, w=s_xqT, wv=pV.v, ws=pS.v):
                    e.wait_ge(pV.h, max(w, wv))
                    e.wait_ge(pS.h, ws)
                    for ct in range(8):
                        e.matmul(psBIG[:, 0:icw], xqTb[:, ct * TCH + tt * 128: ct * TCH + tt * 128 + 128],
                                 wgqT[:, ct * IFF + io: ct * IFF + io + icw],
                                 start=(ct == 0), stop=(ct == 7))
                    for ct in range(8):
                        mm = e.matmul(psBIG[:, 512:512 + icw],
                                      xqTb[:, ct * TCH + tt * 128: ct * TCH + tt * 128 + 128],
                                      wuqT[:, ct * IFF + io: ct * IFF + io + icw],
                                      start=(ct == 0), stop=(ct == 7))
                    mark(pT, mm)
                s_gu = pT.v

                @sec("scalar")
                def _(e, icw=icw, w=s_gu):
                    e.wait_ge(pT.h, w)
                    e.activation(sgi[:, 0:icw], psBIG[:, 0:icw], AF.Silu)
                    mark(pS, e.copy(upsb[:, 0:icw], psBIG[:, 512:512 + icw]))
                s_sg2 = pS.v

                @sec("vector")
                def _(e, tt=tt, io=io, icw=icw, w=s_sg2, wt=pT.v):
                    e.wait_ge(pS.h, w)
                    e.wait_ge(pT.h, wt)
                    e.tensor_tensor(sgi[:, 0:icw], sgi[:, 0:icw], upsb[:, 0:icw], ALU.mult)
                    mark(pV, emit_quant(e, QF, sgi[:, 0:icw], icw, hstg[:, 0:icw]))
                s_hq = pV.v
                if "dbg_h" in dbg and tc == 0:
                    @sec("gpsimd")
                    def _(e, tt=tt, io=io, icw=icw, w=s_hq):
                        e.wait_ge(pV.h, w)
                        dma(e, "gpsimd", dbg["dbg_h"][:, tt * IFF + io: tt * IFF + io + icw],
                            sgi[:, 0:icw])
                        dma(e, "gpsimd", dbg["dbg_hq"][:, tt * IFF + io: tt * IFF + io + icw],
                            hstg[:, 0:icw])

                nsub = icw // 128
                emit_tr([(hstg, sub * 128, CH, hqTb, (io // 128 + sub) * TCH + tt * 128, 22 * TCH)
                         for sub in range(nsub)], fp32=False)
        s_hqT = pV.v

        for m in range(8):
            @sec("tensor")
            def _(e, m=m, wh=s_hqT, wv=pV.v, ws=pS.v):
                if m == 0:
                    e.wait_ge(pV.h, max(wh, wv))
                    e.wait_ge(pS.h, ws)
                for it in range(22):
                    mm = e.matmul(psBIG[:, m * TCH:(m + 1) * TCH],
                                  wdqT[:, it * C + m * 128: it * C + (m + 1) * 128],
                                  hqTb[:, it * TCH:(it + 1) * TCH],
                                  start=(it == 0), stop=(it == 21))
                mark(pT, mm)
        s_dn = pT.v

        for m in range(8):
            @sec("scalar")
            def _(e, m=m, w=s_dn, wv=pV.v, wg=qG.v):
                e.wait_ge(pT.h, w)
                e.wait_ge(pV.h, wv)
                if "dbg_dn" in dbg and tc == 0:
                    e.wait_ge(qG.h, wg)
                mark(pS, e.activation(sgi[:, 0:TCH], psBIG[:, m * TCH:(m + 1) * TCH], AF.Silu))
            s_g2 = pS.v
            if "dbg_dn" in dbg and tc == 0:
                @sec("gpsimd")
                def _(e, m=m, w=s_g2):
                    e.wait_ge(pS.h, w)
                    dma(e, "gpsimd", dbg["dbg_dn"][:, m * TCH:(m + 1) * TCH], sgi[:, 0:TCH])

            @sec("vector")
            def _(e, m=m, w=s_g2):
                e.wait_ge(pS.h, w)
                e.tensor_tensor(x2t[:, 0:TCH], s2c[:, m * TCH:(m + 1) * TCH], sgi[:, 0:TCH], ALU.mult)
                mark(pV, e.tensor_tensor(x2t[:, 0:TCH], x2t[:, 0:TCH],
                                         x1c[:, m * TCH:(m + 1) * TCH], ALU.add))
            s_x2 = pV.v
            if "dbg_x2" in dbg and tc == 0:
                @sec("gpsimd")
                def _(e, m=m, w=s_x2):
                    e.wait_ge(pV.h, w)
                    dma(e, "gpsimd", ap(dbg["dbg_x2"], m * TCH, [[8 * TCH, 128], [1, TCH]]), x2t[:, 0:TCH])
            emit_tr([(x2t, tt * 128, C, xtk, tt * C + m * 128, 2 * C) for tt in range(2)],
                    fp32=True)
        s_ot = pV.v

        @sec("sync")
        def _(e, cl=cl, w=s_ot):
            e.wait_ge(pV.h, w)
            dma(e, "sync", y_out[cl:cl + TCH, :].rearrange("(a p) c -> p a c", p=128),
                xtk[:, 0:2 * C].rearrange("p (a c) -> p a c", a=2))

    ff.close()
    phase_barrier()

    @sec("sync")
    def _(e, w=qS.v):
        e.wait_ge(qS.h, w)

    es.close()
    return nc


def _bucket(rel):
    import math
    try:
        import jax
        import jax.numpy as jnp
        cpu = jax.devices("cpu")[0]
    except Exception:
        jax = None
    if jax is not None:
        with jax.default_device(cpu):
            relative_position = jnp.asarray(rel, dtype=jnp.int32)
            num_buckets = 32
            ret = jnp.zeros_like(relative_position)
            n = -relative_position
            num_buckets //= 2
            ret = ret + (n < 0).astype(jnp.int32) * num_buckets
            n = jnp.abs(n)
            max_exact = num_buckets // 2
            is_small = n < max_exact
            val_if_large = max_exact + (
                jnp.log(jnp.maximum(n, 1).astype(jnp.float32) / max_exact)
                / math.log(128 / max_exact) * (num_buckets - max_exact)).astype(jnp.int32)
            val_if_large = jnp.minimum(val_if_large, num_buckets - 1)
            return np.asarray(ret + jnp.where(is_small, n, val_if_large))
    if True:
        num_buckets = 16
        ret = (rel > 0).astype(np.int64) * num_buckets
        n = np.abs(rel)
        max_exact = num_buckets // 2
        is_small = n < max_exact
        val_if_large = max_exact + (
            np.log(np.maximum(n, 1).astype(np.float32) / max_exact)
            / np.float32(math.log(128 / max_exact)) * (num_buckets - max_exact)).astype(np.int64)
        val_if_large = np.minimum(val_if_large, num_buckets - 1)
        return ret + np.where(is_small, n, val_if_large)


def _prep_core_inputs(core, I):
    b, h = core // 2, core % 2
    t0 = h * TH
    x = np.asarray(I["L0_attn_x"][b], dtype=np.float32)
    d = {}
    d["x_full"] = np.ascontiguousarray(x)
    prev = np.zeros((1, C), np.float32) if t0 == 0 else x[t0 - 1:t0]
    d["x_half"] = np.ascontiguousarray(np.concatenate(
        [prev, x[t0:t0 + TH], np.zeros((17 * 128 - TH - 1, C), np.float32)], axis=0))
    d["eye_f"] = np.eye(128, dtype=np.float32)
    wx = np.asarray(I["L0_attn_Wx"], np.float32)
    nsl = slice(8 * h, 8 * h + 8)
    d["wxt"] = np.ascontiguousarray(np.concatenate(
        [wx[32:, :], wx[0:16, :][nsl], wx[16:32, :][nsl]], axis=0).T)
    d["a_log"] = np.ascontiguousarray(np.asarray(I["L0_attn_A_log"], np.float32)[nsl][None, :])
    d["d_vec"] = np.ascontiguousarray(np.asarray(I["L0_attn_D"], np.float32).reshape(8, 128).T)
    d["mur"] = np.ascontiguousarray(np.asarray(I["L0_ffn_mu_r"], np.float32).reshape(8, 128).T)
    d["muk"] = np.ascontiguousarray(np.asarray(I["L0_ffn_mu_k"], np.float32).reshape(8, 128).T)
    d["wrt"] = np.ascontiguousarray(np.asarray(I["L0_ffn_W_r"], np.float32).T)
    d["wkt"] = np.ascontiguousarray(np.asarray(I["L0_ffn_W_k"], np.float32).T)
    d["wvt"] = np.ascontiguousarray(np.asarray(I["L0_ffn_W_v"], np.float32).T)
    d["woutt"] = np.ascontiguousarray(np.asarray(I["L0_attn_Wout"], np.float32).T)
    hsl = slice(8 * h * DK, 8 * h * DK + 512)
    d["wqt"] = np.ascontiguousarray(np.asarray(I["L1_attn_w_q"], np.float32)[hsl].T)
    d["wktA"] = np.ascontiguousarray(np.asarray(I["L1_attn_w_k"], np.float32)[hsl].T)
    d["wvtA"] = np.ascontiguousarray(np.asarray(I["L1_attn_w_v"], np.float32)[hsl].T)
    d["wot"] = np.ascontiguousarray(np.asarray(I["L1_attn_w_o"], np.float32).T[hsl])
    tab = np.asarray(I["L1_attn_rel_bias_table"], np.float32)
    p = np.arange(128)[:, None]
    m = np.arange(STRIPW)[None, :]
    buck = _bucket(p - m + 4095)
    st = np.empty((NH, 128, STRIPW), np.float32)
    for hl in range(NH):
        st[hl] = tab[buck, 8 * h + hl]
    d["strips"] = np.ascontiguousarray(st)
    wg = np.asarray(I["L1_ffn_w_gate"], np.float32)
    wu = np.asarray(I["L1_ffn_w_up"], np.float32)
    wd = np.asarray(I["L1_ffn_w_down"], np.float32)
    g = b
    wg_sl = np.zeros((WGR, C), np.float32)
    wu_sl = np.zeros((WGR, C), np.float32)
    r0 = WGR * g
    nr = max(0, min(WGR, IFF - r0))
    if nr > 0:
        wg_sl[:nr] = wg[r0:r0 + nr]
        wu_sl[:nr] = wu[r0:r0 + nr]
    d["wg_sl"] = wg_sl
    d["wu_sl"] = wu_sl
    d["wd_sl"] = np.ascontiguousarray(wd[WDR * g:WDR * g + WDR])
    return d


def kernel(**inputs):
    nc = build_nc()
    in_maps = [_prep_core_inputs(c, inputs) for c in range(8)]
    res = run_bass_kernel_spmd(nc, in_maps, core_ids=list(range(8)))
    out = np.empty((B, T, C), np.float32)
    for c in range(8):
        b, h = c // 2, c % 2
        out[b, h * TH:(h + 1) * TH] = res.results[c]["y_out"]
    return out
